# version 01
# baseline (speedup 1.0000x reference)
import contextlib
import math
import numpy as np
import ml_dtypes
import concourse.bass as bass
import concourse.mybir as mybir
from concourse.bass_utils import run_bass_kernel_spmd

F32 = mybir.dt.float32
BF16 = mybir.dt.bfloat16
AF = mybir.ActivationFunctionType
ALU = mybir.AluOpType
ds = bass.ds
BF = ml_dtypes.bfloat16

NCORES = 8
B, T, D = 4, 2048, 2048
TPC = 1024
NCH = D // 128
RS = 1.0 / math.sqrt(128.0)


class EB:
    def __init__(self, ctx, nc, name):
        self.h = ctx.enter_context(nc.semaphore(f"ch_{name}"))
        self.v = 0
        self.ops = []
        self.name = name

    def emit(self, fn, n=1):
        self.v += n
        h, k = self.h, n

        def op(e):
            fn(e).then_inc(h, k)
        self.ops.append(op)
        return self.v

    def raw(self, fn):
        self.ops.append(fn)

    def wait(self, other, v):
        h = other.h
        self.ops.append(lambda e: e.wait_ge(h, v))

    def dma(self, fn):
        return self.emit(fn, 16)

    def run(self, e):
        for f in self.ops:
            f(e)


class Prog:
    def __init__(self, nc, ctx):
        self.nc = nc
        self.ctx = ctx
        self.sy = EB(ctx, nc, "sy")
        self.gp = EB(ctx, nc, "gp")
        self.ve = EB(ctx, nc, "ve")
        self.sc = EB(ctx, nc, "sc")
        self.te = EB(ctx, nc, "te")

    def finish(self):
        with self.nc.Block() as block:
            block.sync(self.sy.run)
            block.gpsimd(self.gp.run)
            block.vector(self.ve.run)
            block.scalar(self.sc.run)
            block.tensor(self.te.run)


def emit_ln(p, nc, xsb, W, consts, out_bf, deps, tag="", alloc_ctx=None,
            outW=None, out_off=0):
    if outW is None:
        outW = W
    wl = []
    off = 0
    while off < W:
        ln_ = min(512, W - off)
        wl.append((off, ln_))
        off += ln_
    NW = len(wl)
    sq = alloc_ctx.enter_context(nc.sbuf_tensor(f"ln_sq{tag}", [128, W], F32))
    st = alloc_ctx.enter_context(nc.sbuf_tensor(f"ln_st{tag}", [1, 2 * W], F32))
    abc = alloc_ctx.enter_context(nc.sbuf_tensor(f"ln_ab{tag}", [128, W], F32))
    bbc = alloc_ctx.enter_context(nc.sbuf_tensor(f"ln_bb{tag}", [128, W], F32))
    pctx = alloc_ctx.enter_context(contextlib.ExitStack())
    ps_sum = [pctx.enter_context(nc.psum_tensor(f"lnS{tag}{w}", [1, wl[w][1]],
                                                F32)) for w in range(NW)]
    ps_sq = [pctx.enter_context(nc.psum_tensor(f"lnQ{tag}{w}", [1, wl[w][1]],
                                               F32)) for w in range(NW)]
    ps_bc = [pctx.enter_context(nc.psum_tensor(f"lnB{tag}{w}",
                                               [128, wl[w // 2][1]], F32))
             for w in range(2 * NW)]
    ones_c = consts["ones_col"]
    ones_r = consts["ones_row"]
    eps = consts["eps"]
    ve, sc, te = p.ve, p.sc, p.te

    for (d, v) in deps:
        sc.wait(d, v)
        te.wait(d, v)

    TE0 = te.v
    V_SQ = []
    for cc in range(NCH):
        if cc >= 1:
            sc.wait(te, TE0 + 2 * NW * cc)
        V_SQ.append(sc.emit(lambda e, cc=cc: e.activation(
            sq[:, :], xsb[:, cc * W:(cc + 1) * W], AF.Square)))
    for cc in range(NCH):
        te.wait(sc, V_SQ[cc])
        for w, (wo, wn_) in enumerate(wl):
            te.emit(lambda e, cc=cc, w=w, wo=wo, wn_=wn_: e.matmul(
                ps_sum[w][:, :], ones_c[:, 0:1],
                xsb[:, cc * W + wo: cc * W + wo + wn_],
                start=(cc == 0), stop=(cc == NCH - 1)))
            te.emit(lambda e, cc=cc, w=w, wo=wo, wn_=wn_: e.matmul(
                ps_sq[w][:, :], ones_c[:, 0:1],
                sq[:, wo: wo + wn_],
                start=(cc == 0), stop=(cc == NCH - 1)))
    V_MM = te.v
    assert V_MM == TE0 + 2 * NW * NCH

    sc.wait(te, V_MM)
    V_MU = None
    for w, (wo, wn_) in enumerate(wl):
        V_MU = sc.emit(lambda e, w=w, wo=wo, wn_=wn_: e.mul(
            st[0:1, wo: wo + wn_], ps_sum[w][:, :], 1.0 / D))
    sc.wait(sc, V_MU)
    V_M2 = sc.emit(lambda e: e.activation(st[0:1, W:2 * W], st[0:1, 0:W],
                                          AF.Square))
    ve.wait(sc, V_M2)
    V_VAR = None
    for w, (wo, wn_) in enumerate(wl):
        if w >= 1:
            ve.wait(ve, V_VAR)
        V_VAR = ve.emit(lambda e, w=w, wo=wo, wn_=wn_: e.scalar_tensor_tensor(
            st[0:1, W + wo: W + wo + wn_],
            ps_sq[w][:, :], 1.0 / D,
            st[0:1, W + wo: W + wo + wn_],
            ALU.mult, ALU.subtract))
    sc.wait(ve, V_VAR)
    V_LN1 = sc.emit(lambda e: e.activation(st[0:1, W:2 * W],
                                           st[0:1, W:2 * W], AF.Ln,
                                           bias=eps[0:1, 0:1]))
    sc.wait(sc, V_LN1)
    V_RSTD = sc.emit(lambda e: e.activation(st[0:1, W:2 * W],
                                            st[0:1, W:2 * W],
                                            AF.Exp, scale=-0.5))
    ve.wait(sc, V_RSTD)
    V_BROW = ve.emit(lambda e: e.scalar_tensor_tensor(
        st[0:1, 0:W], st[0:1, 0:W], -1.0, st[0:1, W:2 * W],
        ALU.mult, ALU.mult))
    te.wait(ve, V_BROW)
    V_BC = None
    for w, (wo, wn_) in enumerate(wl):
        V_BC = te.emit(lambda e, w=w, wo=wo, wn_=wn_: e.matmul(
            ps_bc[2 * w][:, :], ones_r[0:1, :],
            st[0:1, W + wo: W + wo + wn_],
            start=True, stop=True))
        V_BC = te.emit(lambda e, w=w, wo=wo, wn_=wn_: e.matmul(
            ps_bc[2 * w + 1][:, :], ones_r[0:1, :],
            st[0:1, wo: wo + wn_],
            start=True, stop=True))
    ve.wait(te, V_BC)
    for w, (wo, wn_) in enumerate(wl):
        ve.emit(lambda e, w=w, wo=wo, wn_=wn_: e.tensor_copy(
            abc[:, wo: wo + wn_], ps_bc[2 * w][:, :]))
        ve.emit(lambda e, w=w, wo=wo, wn_=wn_: e.tensor_copy(
            bbc[:, wo: wo + wn_], ps_bc[2 * w + 1][:, :]))
    V_CP = ve.v
    V_DONE = None
    for cc in range(NCH):
        if cc == 0:
            ve.wait(ve, V_CP)
        else:
            ve.wait(ve, V_DONE)
        ve.emit(lambda e, cc=cc: e.tensor_mul(
            sq[:, :], xsb[:, cc * W:(cc + 1) * W], abc[:, :]))
        ve.wait(ve, ve.v)
        V_DONE = ve.emit(lambda e, cc=cc: e.tensor_add(
            out_bf[:, cc * outW + out_off: cc * outW + out_off + W],
            sq[:, :], bbc[:, :]))
    pctx.close()
    return V_DONE, st



def build_attnA1():
    nc = bass.Bass("TRN2", target_bir_lowering=False)
    xT = nc.dram_tensor("xT", [D, TPC], F32, kind="ExternalInput").ap()
    wqkvT = nc.dram_tensor("wqkvT", [D, 3 * D], F32, kind="ExternalInput").ap()
    mask = nc.dram_tensor("mask", [128, 128], F32, kind="ExternalInput").ap()
    identb = nc.dram_tensor("identb", [128, 128], F32, kind="ExternalInput").ap()
    onescol = nc.dram_tensor("onescol", [128, 2], F32, kind="ExternalInput").ap()
    onesrow = nc.dram_tensor("onesrow", [1, 128], F32, kind="ExternalInput").ap()
    att = nc.dram_tensor("att", [16 * TPC, 128], BF16, kind="ExternalOutput").ap()
    dq = nc.dram_tensor("dq", [TPC, D], BF16)
    dk = nc.dram_tensor("dk", [TPC, D], BF16)
    dv = nc.dram_tensor("dv", [TPC, D], BF16)

    with contextlib.ExitStack() as ctx:
        p = Prog(nc, ctx)
        sy, gp, ve, sc, te = p.sy, p.gp, p.ve, p.sc, p.te
        ones_c = ctx.enter_context(nc.sbuf_tensor("onesc", [128, 2], F32))
        ones_r = ctx.enter_context(nc.sbuf_tensor("onesr", [1, 128], F32))
        ident_bf = ctx.enter_context(nc.sbuf_tensor("identbf", [128, 128], BF16))
        mask_sb = ctx.enter_context(nc.sbuf_tensor("masksb", [128, 128], F32))
        xhat = ctx.enter_context(nc.sbuf_tensor("xhat", [128, NCH * TPC], BF16))
        consts = {"ones_col": ones_c[:, 0:1], "ones_row": ones_r,
                  "eps": ones_c[:, 1:2]}

        p1 = ctx.enter_context(contextlib.ExitStack())
        xsb = p1.enter_context(nc.sbuf_tensor("xsb", [128, NCH * TPC], F32))

        gp.dma(lambda e: e.dma_start(
            out=xsb.rearrange("p (cc t) -> p cc t", cc=NCH),
            in_=xT.rearrange("(cc p) t -> p cc t", p=128)))
        gp.dma(lambda e: e.dma_start(out=ones_c[:, :], in_=onescol))
        gp.dma(lambda e: e.dma_start(out=ones_r[:, :], in_=onesrow))
        gp.dma(lambda e: e.dma_start(out=ident_bf[:, :], in_=identb))
        V_IN = gp.dma(lambda e: e.dma_start(out=mask_sb[:, :], in_=mask))

        V_LN, _ = emit_ln(p, nc, xsb, TPC, consts, xhat, [(gp, V_IN)],
                          tag="a", alloc_ctx=p1)
        p1.close()

        p2 = ctx.enter_context(contextlib.ExitStack())
        wbuf = p2.enter_context(nc.sbuf_tensor("wbuf", [128, 2 * 512], BF16))
        stage = p2.enter_context(nc.sbuf_tensor("stage", [128, 8 * 512], BF16))
        ps_qkv = [p2.enter_context(nc.psum_tensor(f"psq{i}", [128, 512], F32))
                  for i in range(8)]
        NFG = 12
        GP0, TE0, SC0, SY0 = gp.v, te.v, sc.v, sy.v
        def te_after(fg, cc):
            return TE0 + 8 * (fg * NCH + cc + 1)

        gp.wait(ve, V_LN)
        for fg in range(NFG):
            for cc in range(NCH):
                i = fg * NCH + cc
                if i >= 2:
                    fg2, cc2 = divmod(i - 2, NCH)
                    gp.wait(te, te_after(fg2, cc2))
                gp.dma(lambda e, fg=fg, cc=cc, i=i: e.dma_start(
                    out=wbuf[:, (i % 2) * 512:(i % 2 + 1) * 512],
                    in_=wqkvT[cc * 128:(cc + 1) * 128,
                              fg * 512:(fg + 1) * 512]))

        te.wait(ve, V_LN)
        for fg in range(NFG):
            if fg >= 1:
                te.wait(sc, SC0 + 8 * fg)
            for cc in range(NCH):
                i = fg * NCH + cc
                te.wait(gp, GP0 + 16 * min(i + 2, NFG * NCH))
                for tc in range(8):
                    te.emit(lambda e, i=i, cc=cc, tc=tc: e.matmul(
                        ps_qkv[tc][:, :],
                        xhat[:, cc * TPC + tc * 128: cc * TPC + (tc + 1) * 128],
                        wbuf[:, (i % 2) * 512:(i % 2 + 1) * 512],
                        start=(cc == 0), stop=(cc == NCH - 1)))
                assert te.v == te_after(fg, cc)

        for fg in range(NFG):
            sc.wait(te, te_after(fg, NCH - 1))
            sc.wait(sy, SY0 + 16 * fg)
            for tc in range(8):
                sc.emit(lambda e, tc=tc: e.activation(
                    stage[:, tc * 512:(tc + 1) * 512], ps_qkv[tc][:, :],
                    AF.Identity))
            assert sc.v == SC0 + 8 * (fg + 1)

        for fg in range(NFG):
            sy.wait(sc, SC0 + 8 * (fg + 1))
            sy.wait(sy, SY0 + 16 * fg)
            tgt = [dq, dk, dv][fg // 4]
            col = (fg % 4) * 512
            sy.dma(lambda e, tgt=tgt, col=col: e.dma_start(
                out=tgt.ap()[0:TPC, col:col + 512].rearrange(
                    "(tc p) f -> p tc f", p=128),
                in_=stage.rearrange("p (tc f) -> p tc f", f=512)))
        V_ST = sy.v
        assert V_ST == SY0 + 16 * NFG
        p2.close()

        p3 = ctx.enter_context(contextlib.ExitStack())
        qTs = p3.enter_context(nc.sbuf_tensor("qTs", [128, 16 * TPC], BF16))
        kTs = p3.enter_context(nc.sbuf_tensor("kTs", [128, 16 * TPC], BF16))
        vS = p3.enter_context(nc.sbuf_tensor("vS", [128, 16 * TPC], BF16))
        ps_sc = p3.enter_context(nc.psum_tensor("pssc", [128, 128], F32))
        ps_tr = p3.enter_context(nc.psum_tensor("pstr", [128, 128], BF16))
        ps_o = p3.enter_context(nc.psum_tensor("pso", [128, 128], F32))
        scf = p3.enter_context(nc.sbuf_tensor("scf", [128, 128], F32))
        expb = p3.enter_context(nc.sbuf_tensor("expb", [128, 128], BF16))
        expT = p3.enter_context(nc.sbuf_tensor("expT", [128, 128], BF16))
        den = p3.enter_context(nc.sbuf_tensor("den", [128, 2], F32))
        ostg = p3.enter_context(nc.sbuf_tensor("ostg", [128, 16 * 128], BF16))
        NG = 16 * TPC // 128

        sy.wait(sy, V_ST)
        sy.dma(lambda e: e.dma_start_transpose(
            qTs[:, :], dq.ap().rearrange("t (i d) -> (t i) d", d=128)))
        sy.dma(lambda e: e.dma_start_transpose(
            kTs[:, :], dk.ap().rearrange("t (i d) -> (t i) d", d=128)))
        for tp in range(8):
            sy.dma(lambda e, tp=tp: e.dma_start(
                out=vS[tp * 16:(tp + 1) * 16, :].rearrange(
                    "i (g d) -> i g d", d=128),
                in_=bass.AP(dv, tp * D, [[128, 16], [8 * D, 128], [1, 128]])))
        V_TR = sy.v

        TE1, VE1, SC1, SY1 = te.v, ve.v, sc.v, sy.v
        te.wait(sy, V_TR)
        for g in range(NG):
            if g >= 1:
                te.wait(ve, VE1 + 3 * g)
            te.emit(lambda e, g=g: e.matmul(
                ps_sc[:, :], qTs[:, g * 128:(g + 1) * 128],
                kTs[:, g * 128:(g + 1) * 128], start=True, stop=True))
            te.wait(sc, SC1 + 2 * g + 1)
            te.emit(lambda e: e.transpose(ps_tr[:, :], expb[:, :],
                                          ident_bf[:, :]))
            te.wait(ve, VE1 + 3 * g + 2)
            te.emit(lambda e, g=g: e.matmul(
                ps_o[:, :], expT[:, :], vS[:, g * 128:(g + 1) * 128],
                start=True, stop=True))

        for g in range(NG):
            ve.wait(te, TE1 + 3 * g + 1)
            if g >= 1:
                ve.wait(sc, SC1 + 2 * g)
            ve.emit(lambda e: e.scalar_tensor_tensor(
                scf[:, :], ps_sc[:, :], RS, mask_sb[:, :], ALU.mult, ALU.add))
            ve.wait(te, TE1 + 3 * g + 2)
            ve.emit(lambda e: e.tensor_copy(expT[:, :], ps_tr[:, :]))
            ve.wait(ve, VE1 + 3 * g + 2)
            ve.wait(sc, SC1 + 2 * g + 1)
            ve.emit(lambda e: e.reciprocal(den[:, 1:2], den[:, 0:1]))

        for g in range(NG):
            sc.wait(ve, VE1 + 3 * g + 1)
            if g >= 1:
                sc.wait(te, TE1 + 3 * g - 1)
                sc.wait(ve, VE1 + 3 * g)
            sc.emit(lambda e: e.activation(expb[:, :], scf[:, :], AF.Exp,
                                           accum_out=den[:, 0:1]))
            sc.wait(te, TE1 + 3 * g + 3)
            sc.wait(ve, VE1 + 3 * g + 3)
            if g >= 16:
                sc.wait(sy, SY1 + 16 * (g // 16))
            sc.emit(lambda e, g=g: e.activation(
                ostg[:, (g % 16) * 128:(g % 16 + 1) * 128], ps_o[:, :],
                AF.Identity, scale=den[:, 1:2]))

        for blk in range(NG // 16):
            sy.wait(sc, SC1 + 2 * 16 * (blk + 1))
            sy.wait(sy, V_TR + 16 * blk)
            sy.dma(lambda e, blk=blk: e.dma_start(
                out=att[blk * 2048:(blk + 1) * 2048, :].rearrange(
                    "(g p) d -> p g d", p=128),
                in_=ostg.rearrange("p (g d) -> p g d", d=128)))
        sy.wait(sy, sy.v)
        p3.close()

        p.finish()
    return nc



def emit_ffcols(p, nc, ctx, src_bf, srcW, w_dram, out_bf, outW, act, nfc,
                ncc=NCH, tag="f", src_off=0, L=None, out_off=0, addto=None,
                addW=None, add_off=0, multo=False):
    if L is None:
        L = srcW
    wl = []
    off = 0
    while off < L:
        wn_ = min(512, L - off)
        wl.append((off, wn_))
        off += wn_
    NT = len(wl)
    ictx = ctx.enter_context(contextlib.ExitStack())
    wcol = ictx.enter_context(nc.sbuf_tensor(f"wc{tag}", [128, 2 * ncc * 128],
                                             BF16))
    pst = [ictx.enter_context(nc.psum_tensor(f"pf{tag}{i}",
                                             [128, wl[i % NT][1]], F32))
           for i in range(2 * NT)]
    gp, te, sc, ve = p.gp, p.te, p.sc, p.ve
    ev = ve if addto is not None else sc
    GP0 = gp.v
    TE0 = te.v
    EV0 = ev.v
    gp.wait(gp, GP0)
    gp.wait(te, TE0)
    te.wait(sc, sc.v)
    te.wait(ve, ve.v)

    def te_after(fc):
        return TE0 + (fc + 1) * NT * ncc

    for fc in range(nfc):
        if fc >= 2:
            gp.wait(te, te_after(fc - 2))
        gp.dma(lambda e, fc=fc: e.dma_start(
            out=wcol.rearrange("p (s cc f) -> p s cc f", s=2, f=128)
                [:, fc % 2],
            in_=w_dram[:, fc * 128:(fc + 1) * 128].rearrange(
                "(cc p) f -> p cc f", p=128)))

    for fc in range(nfc):
        te.wait(gp, GP0 + 16 * min(fc + 2, nfc))
        if fc >= 2:
            te.wait(ev, EV0 + NT * (fc - 1))
        for tt, (wo, wn_) in enumerate(wl):
            for cc in range(ncc):
                te.emit(lambda e, fc=fc, tt=tt, cc=cc, wo=wo, wn_=wn_:
                        e.matmul(
                    pst[(fc % 2) * NT + tt][:, :],
                    wcol[:, (fc % 2) * ncc * 128 + cc * 128:
                         (fc % 2) * ncc * 128 + (cc + 1) * 128],
                    src_bf[:, cc * srcW + src_off + wo:
                           cc * srcW + src_off + wo + wn_],
                    start=(cc == 0), stop=(cc == ncc - 1)))

    for fc in range(nfc):
        for tt, (wo, wn_) in enumerate(wl):
            ev.wait(te, TE0 + fc * NT * ncc + (tt + 1) * ncc)
            if addto is not None:
                ev.wait(ve, ve.v)
                op = ALU.mult if multo else ALU.add
                ve.emit(lambda e, fc=fc, tt=tt, wo=wo, wn_=wn_, op=op:
                        e.tensor_tensor(
                    addto[:, fc * addW + add_off + wo:
                          fc * addW + add_off + wo + wn_],
                    pst[(fc % 2) * NT + tt][:, :],
                    addto[:, fc * addW + add_off + wo:
                          fc * addW + add_off + wo + wn_], op))
            else:
                sc.emit(lambda e, fc=fc, tt=tt, wo=wo, wn_=wn_: e.activation(
                    out_bf[:, fc * outW + out_off + wo:
                           fc * outW + out_off + wo + wn_],
                    pst[(fc % 2) * NT + tt][:, :], act))
    assert ev.v == EV0 + nfc * NT
    ictx.close()
    return ev.v, NT


def emit_router(p, nc, ctx, xsb, stats, consts, E, wrT_dram, wn_out, tag="r"):
    W = TPC
    wrs = ctx.enter_context(nc.sbuf_tensor(f"wrs{tag}", [128, NCH * E], F32))
    cols = ctx.enter_context(nc.sbuf_tensor(f"cols{tag}", [128, 2 * 8], F32))
    csbc = ctx.enter_context(nc.sbuf_tensor(f"csbc{tag}", [128, E], F32))
    lrow = ctx.enter_context(nc.sbuf_tensor(f"lrow{tag}", [1, E], F32))
    sml = ctx.enter_context(nc.sbuf_tensor(f"sml{tag}", [128, 2 * E + 8], F32))
    pctx = ctx.enter_context(contextlib.ExitStack())
    ps_l = pctx.enter_context(nc.psum_tensor(f"psl{tag}", [128, E], F32))
    ps_cs = pctx.enter_context(nc.psum_tensor(f"pcs{tag}", [1, E], F32))
    ps_cb = pctx.enter_context(nc.psum_tensor(f"pcb{tag}", [128, E], F32))
    ps_t = pctx.enter_context(nc.psum_tensor(f"pst{tag}", [128, 2], F32))
    gp, te, sc, ve = p.gp, p.te, p.sc, p.ve
    ones_c, ones_r = consts["ones_col"], consts["ones_row"]

    gp.wait(gp, gp.v)
    V_WR = gp.dma(lambda e: e.dma_start(
        out=wrs.rearrange("p (cc f) -> p cc f", f=E),
        in_=wrT_dram.rearrange("(cc p) f -> p cc f", p=128)))

    te.wait(gp, V_WR)
    V_CS = None
    for cc in range(NCH):
        V_CS = te.emit(lambda e, cc=cc: e.matmul(
            ps_cs[:, :], ones_c[:, 0:1], wrs[:, cc * E:(cc + 1) * E],
            start=(cc == 0), stop=(cc == NCH - 1)))
    sc.wait(te, V_CS)
    V_CSR = sc.emit(lambda e: e.activation(lrow[0:1, :], ps_cs[:, :],
                                           AF.Identity))
    te.wait(sc, V_CSR)
    V_CB = te.emit(lambda e: e.matmul(ps_cb[:, :], ones_r[0:1, :],
                                      lrow[0:1, :], start=True, stop=True))
    ve.wait(te, V_CB)
    V_CBE = ve.emit(lambda e: e.tensor_copy(csbc[:, :], ps_cb[:, :]))

    V_TC = []
    for tc in range(8):
        V1 = te.emit(lambda e, tc=tc: e.transpose(
            ps_t[:, 0:1], stats[0:1, W + tc * 128: W + (tc + 1) * 128],
            consts["one1"][0:1, 0:1]))
        V1 = te.emit(lambda e, tc=tc: e.transpose(
            ps_t[:, 1:2], stats[0:1, tc * 128:(tc + 1) * 128],
            consts["one1"][0:1, 0:1]))
        ve.wait(te, V1)
        V_TC.append(ve.emit(lambda e, tc=tc: e.tensor_copy(
            cols[:, 2 * tc:2 * tc + 2], ps_t[:, 0:2])))
        if tc < 7:
            te.wait(ve, V_TC[-1])
    V_OUT = None
    V_TS1 = None
    for tc in range(8):
        if tc >= 1:
            te.wait(ve, V_TS1)
        V_L = None
        for cc in range(NCH):
            V_L = te.emit(lambda e, tc=tc, cc=cc: e.matmul(
                ps_l[:, :],
                xsb[:, cc * W + tc * 128: cc * W + (tc + 1) * 128],
                wrs[:, cc * E:(cc + 1) * E],
                start=(cc == 0), stop=(cc == NCH - 1)))
        ve.wait(te, V_L)
        ve.wait(ve, max(V_CBE, V_TC[tc]))
        if tc >= 1:
            ve.wait(ve, V_OUT)
        V_TS1 = ve.emit(lambda e, tc=tc: e.tensor_scalar(
            sml[:, 0:E], ps_l[:, :], cols[:, 2 * tc:2 * tc + 1], None,
            ALU.mult))
        ve.wait(ve, ve.v)
        ve.emit(lambda e, tc=tc: e.scalar_tensor_tensor(
            sml[:, 0:E], csbc[:, :], cols[:, 2 * tc + 1:2 * tc + 2],
            sml[:, 0:E], ALU.mult, ALU.add))
        ve.wait(ve, ve.v)
        ve.emit(lambda e: e.tensor_reduce(
            sml[:, 2 * E:2 * E + 1], sml[:, 0:E], mybir.AxisListType.X,
            ALU.max, negate=True))
        ve.wait(ve, ve.v)
        V_NM = ve.v
        sc.wait(ve, V_NM)
        V_EX = sc.emit(lambda e: e.activation(
            sml[:, E:2 * E], sml[:, 0:E], AF.Exp,
            bias=sml[:, 2 * E:2 * E + 1]))
        ve.wait(sc, V_EX)
        ve.emit(lambda e: e.tensor_scalar(
            sml[:, 0:E], sml[:, E:2 * E], 1.0, None, ALU.is_equal))
        ve.wait(ve, ve.v)
        ve.emit(lambda e: e.scalar_tensor_tensor(
            sml[:, 0:E], sml[:, 0:E], -30000.0, sml[:, E:2 * E],
            ALU.mult, ALU.add))
        ve.wait(ve, ve.v)
        ve.emit(lambda e: e.tensor_reduce(
            sml[:, 2 * E + 1:2 * E + 2], sml[:, 0:E], mybir.AxisListType.X,
            ALU.max))
        ve.wait(ve, ve.v)
        ve.emit(lambda e: e.tensor_scalar(
            sml[:, 0:E], sml[:, E:2 * E], sml[:, 2 * E + 1:2 * E + 2], None,
            ALU.is_ge))
        ve.wait(ve, ve.v)
        ve.emit(lambda e: e.scalar_tensor_tensor(
            sml[:, 0:E], sml[:, E:2 * E], 1.0, sml[:, 0:E],
            ALU.mult, ALU.mult, accum_out=sml[:, 2 * E + 2:2 * E + 3]))
        ve.wait(ve, ve.v)
        ve.emit(lambda e: e.reciprocal(sml[:, 2 * E + 3:2 * E + 4],
                                       sml[:, 2 * E + 2:2 * E + 3]))
        ve.wait(ve, ve.v)
        V_OUT = ve.emit(lambda e, tc=tc: e.tensor_scalar(
            wn_out[:, tc * E:(tc + 1) * E], sml[:, 0:E],
            sml[:, 2 * E + 3:2 * E + 4], None, ALU.mult))
    pctx.close()
    return V_OUT


def emit_ffn0(p, nc, ctx, xsb, consts, w1T, w3T, w2T, wrT, part_out,
              deps, tag="n"):
    gp, te, sc, ve, sy = p.gp, p.te, p.sc, p.ve, p.sy
    fctx = ctx.enter_context(contextlib.ExitStack())
    wn = fctx.enter_context(nc.sbuf_tensor(f"wn{tag}", [128, 8 * 8], BF16))
    p8 = fctx.enter_context(nc.sbuf_tensor(f"p8{tag}", [8, 2048], F32))
    h2 = fctx.enter_context(nc.sbuf_tensor(f"h2{tag}", [128, NCH * TPC], BF16))
    h3t = fctx.enter_context(nc.sbuf_tensor(f"h3{tag}", [128, 2048], BF16))
    ictx = fctx.enter_context(contextlib.ExitStack())
    h1 = ictx.enter_context(nc.sbuf_tensor(f"h1{tag}", [128, NCH * TPC], BF16))
    xctx = ictx.enter_context(contextlib.ExitStack())
    xh = xctx.enter_context(nc.sbuf_tensor(f"xh{tag}", [128, NCH * TPC], BF16))
    V_LN, stats = emit_ln(p, nc, xsb, TPC, consts, xh, deps, tag=tag,
                          alloc_ctx=xctx)
    for eng in (p.te, p.sc, p.gp):
        eng.wait(ve, V_LN)
    V_RT = emit_router(p, nc, xctx, xsb, stats, consts, 8, wrT, wn, tag=tag)
    p.te.wait(ve, V_RT)
    emit_ffcols(p, nc, xctx, xh, TPC, w1T, h1, TPC, AF.Silu, NCH,
                tag=tag + "1")
    V_H1S = sc.v
    xctx.close()
    for eng in (p.te, p.gp):
        eng.wait(sc, V_H1S)
    emit_ffcols(p, nc, ictx, h1, TPC, w3T, h2, TPC, AF.Silu, NCH,
                tag=tag + "3")
    V_H2S = sc.v
    ictx.close()

    w2r = fctx.enter_context(nc.sbuf_tensor(f"w2r{tag}", [128, NCH * 2048],
                                            BF16))
    pctx = fctx.enter_context(contextlib.ExitStack())
    ps3 = [pctx.enter_context(nc.psum_tensor(f"p3{tag}{i}", [128, 512], F32))
           for i in range(2)]
    ps8 = [pctx.enter_context(nc.psum_tensor(f"p8{tag}{i}", [8, 512], F32))
           for i in range(4)]
    gp.wait(gp, gp.v)
    gp.wait(sc, V_H2S)
    for fc in range(NCH):
        gp.dma(lambda e, fc=fc: e.dma_start(
            out=w2r.rearrange("p (fc c) -> p fc c", c=2048)[:, fc],
            in_=w2T[fc * 128:(fc + 1) * 128, :]))
    V_W2 = gp.v

    te.wait(gp, V_W2)
    te.wait(sc, V_H2S)
    TE3 = te.v
    SC3 = sc.v
    VE3 = ve.v

    for tc in range(8):
        te.wait(ve, V_RT)
        for cs in range(4):
            k = tc * 4 + cs
            if k >= 2:
                te.wait(sc, SC3 + (k - 1))
            for fc in range(NCH):
                te.emit(lambda e, tc=tc, cs=cs, fc=fc, k=k: e.matmul(
                    ps3[k % 2][:, :],
                    h2[:, fc * TPC + tc * 128: fc * TPC + (tc + 1) * 128],
                    w2r[:, fc * 2048 + cs * 512: fc * 2048 + (cs + 1) * 512],
                    start=(fc == 0), stop=(fc == NCH - 1)))
            sc.wait(te, te.v)
            sc.emit(lambda e, cs=cs, k=k: e.activation(
                h3t[:, cs * 512:(cs + 1) * 512], ps3[k % 2][:, :],
                AF.Identity))
            te.wait(sc, sc.v)
            te.emit(lambda e, tc=tc, cs=cs: e.matmul(
                ps8[cs][:, :], wn[:, tc * 8:(tc + 1) * 8],
                h3t[:, cs * 512:(cs + 1) * 512],
                start=(tc == 0), stop=(tc == 7)))
    V_TE8 = te.v
    ve.wait(te, V_TE8)
    V_P8 = None
    for cs in range(4):
        V_P8 = ve.emit(lambda e, cs=cs: e.tensor_copy(
            p8[:, cs * 512:(cs + 1) * 512], ps8[cs][:, :]))
    sy.wait(ve, V_P8)
    V_PS = sy.dma(lambda e: e.dma_start(out=part_out, in_=p8[:, :]))
    pctx.close()
    fctx.close()
    return V_PS



def build_F1():
    nc = bass.Bass("TRN2", target_bir_lowering=False)
    xT = nc.dram_tensor("xT", [D, TPC], F32, kind="ExternalInput").ap()
    rowsT = nc.dram_tensor("rowsT", [D, TPC], BF16, kind="ExternalInput").ap()
    wprojT = nc.dram_tensor("wprojT", [D, D], F32, kind="ExternalInput").ap()
    w1T = nc.dram_tensor("w1T", [D, D], F32, kind="ExternalInput").ap()
    w3T = nc.dram_tensor("w3T", [D, D], F32, kind="ExternalInput").ap()
    w2T = nc.dram_tensor("w2T", [D, D], F32, kind="ExternalInput").ap()
    wrT = nc.dram_tensor("wrT", [D, 8], F32, kind="ExternalInput").ap()
    onescol = nc.dram_tensor("onescol", [128, 4], F32, kind="ExternalInput").ap()
    onesrow = nc.dram_tensor("onesrow", [1, 128], F32, kind="ExternalInput").ap()
    x1T = nc.dram_tensor("x1T", [D, TPC], F32, kind="ExternalOutput").ap()
    part = nc.dram_tensor("part", [8, 2048], F32, kind="ExternalOutput").ap()

    with contextlib.ExitStack() as ctx:
        p = Prog(nc, ctx)
        gp, te, sc, ve, sy = p.gp, p.te, p.sc, p.ve, p.sy
        ones_c = ctx.enter_context(nc.sbuf_tensor("onesc", [128, 4], F32))
        ones_r = ctx.enter_context(nc.sbuf_tensor("onesr", [1, 128], F32))
        xsb = ctx.enter_context(nc.sbuf_tensor("xsb", [128, NCH * TPC], F32))
        consts = {"ones_col": ones_c[:, 0:1], "ones_row": ones_r,
                  "eps": ones_c[:, 1:2], "one1": ones_c,
                  "ones_bf": None}

        gp.dma(lambda e: e.dma_start(out=ones_c[:, :], in_=onescol))
        gp.dma(lambda e: e.dma_start(out=ones_r[:, :], in_=onesrow))
        gp.dma(lambda e: e.dma_start(
            out=xsb.rearrange("p (cc t) -> p cc t", cc=NCH),
            in_=xT.rearrange("(cc p) t -> p cc t", p=128)))
        V_IN = gp.dma(lambda e: e.dma_start(
            out=rsb.rearrange("p (cc t) -> p cc t", cc=NCH),
            in_=rowsT.rearrange("(cc p) t -> p cc t", p=128)))

        prctx = ctx.enter_context(contextlib.ExitStack())
        rsb = prctx.enter_context(nc.sbuf_tensor("rsb", [128, NCH * TPC],
                                                 BF16))
        wcol = prctx.enter_context(nc.sbuf_tensor("wcp", [128, 2 * NCH * 128],
                                                  BF16))
        psp = [prctx.enter_context(nc.psum_tensor(f"psp{i}", [128, 512], F32))
               for i in range(4)]
        GP0, TE0, VE0 = gp.v, te.v, ve.v

        def te_after(fc):
            return TE0 + (fc + 1) * 2 * NCH

        gp.wait(gp, V_IN)
        for fc in range(NCH):
            if fc >= 2:
                gp.wait(te, te_after(fc - 2))
            gp.dma(lambda e, fc=fc: e.dma_start(
                out=wcol.rearrange("p (s cc f) -> p s cc f", s=2, f=128)
                    [:, fc % 2],
                in_=wprojT[:, fc * 128:(fc + 1) * 128].rearrange(
                    "(cc p) f -> p cc f", p=128)))
        te.wait(gp, V_IN)
        for fc in range(NCH):
            te.wait(gp, GP0 + 16 * min(fc + 2, NCH))
            if fc >= 2:
                te.wait(ve, VE0 + 2 * (fc - 1))
            for tt in range(2):
                for cc in range(NCH):
                    te.emit(lambda e, fc=fc, tt=tt, cc=cc: e.matmul(
                        psp[(fc % 2) * 2 + tt][:, :],
                        wcol[:, (fc % 2) * NCH * 128 + cc * 128:
                             (fc % 2) * NCH * 128 + (cc + 1) * 128],
                        rsb[:, cc * TPC + tt * 512: cc * TPC + (tt + 1) * 512],
                        start=(cc == 0), stop=(cc == NCH - 1)))
        for fc in range(NCH):
            ve.wait(te, TE0 + fc * 2 * NCH + NCH)
            ve.emit(lambda e, fc=fc: e.tensor_add(
                xsb[:, fc * TPC: fc * TPC + 512], psp[(fc % 2) * 2][:, :],
                xsb[:, fc * TPC: fc * TPC + 512]))
            ve.wait(te, TE0 + (fc + 1) * 2 * NCH)
            ve.emit(lambda e, fc=fc: e.tensor_add(
                xsb[:, fc * TPC + 512: fc * TPC + 1024],
                psp[(fc % 2) * 2 + 1][:, :],
                xsb[:, fc * TPC + 512: fc * TPC + 1024]))
        V_X1 = ve.v
        assert V_X1 == VE0 + 2 * NCH
        prctx.close()

        sy.wait(ve, V_X1)
        V_X1ST = sy.dma(lambda e: e.dma_start(
            out=x1T.rearrange("(cc p) t -> p cc t", p=128),
            in_=xsb.rearrange("p (cc t) -> p cc t", cc=NCH)))

        emit_ffn0(p, nc, ctx, xsb, consts, w1T, w3T, w2T, wrT, part,
                  [(ve, V_X1), (gp, V_IN)], tag="n")
        sy.wait(sy, sy.v)
        p.finish()
    return nc



def build_B1():
    nc = bass.Bass("TRN2", target_bir_lowering=False)
    xT = nc.dram_tensor("xT", [D, TPC], F32, kind="ExternalInput").ap()
    scat = nc.dram_tensor("scat", [D, 64], F32, kind="ExternalInput").ap()
    haloT = nc.dram_tensor("haloT", [D, 256], F32, kind="ExternalInput").ap()
    wqTp = nc.dram_tensor("wqTp", [D, D], F32, kind="ExternalInput").ap()
    wkTp = nc.dram_tensor("wkTp", [D, D], F32, kind="ExternalInput").ap()
    wvT = nc.dram_tensor("wvT", [D, D], F32, kind="ExternalInput").ap()
    woD = nc.dram_tensor("woD", [D, D], F32, kind="ExternalInput").ap()
    tabQ = nc.dram_tensor("tabQ", [128, 2 * TPC], F32,
                          kind="ExternalInput").ap()
    tabK = nc.dram_tensor("tabK", [128, 2 * 1280], F32,
                          kind="ExternalInput").ap()
    maskB = nc.dram_tensor("maskB", [128, 16 * 256], F32,
                           kind="ExternalInput").ap()
    identb = nc.dram_tensor("identb", [128, 128], F32, kind="ExternalInput").ap()
    w1T = nc.dram_tensor("w1T", [D, D], F32, kind="ExternalInput").ap()
    w3T = nc.dram_tensor("w3T", [D, D], F32, kind="ExternalInput").ap()
    w2T = nc.dram_tensor("w2T", [D, D], F32, kind="ExternalInput").ap()
    wrT = nc.dram_tensor("wrT", [D, 8], F32, kind="ExternalInput").ap()
    onescol = nc.dram_tensor("onescol", [128, 4], F32, kind="ExternalInput").ap()
    onesrow = nc.dram_tensor("onesrow", [1, 128], F32, kind="ExternalInput").ap()
    x3T = nc.dram_tensor("x3T", [D, TPC], F32, kind="ExternalOutput").ap()
    part = nc.dram_tensor("part", [8, 2048], F32, kind="ExternalOutput").ap()
    dxx = nc.dram_tensor("dxx", [D, TPC], F32)
    datt = nc.dram_tensor("datt", [D, TPC], BF16)

    KL = 1280

    with contextlib.ExitStack() as ctx:
        p = Prog(nc, ctx)
        gp, te, sc, ve, sy = p.gp, p.te, p.sc, p.ve, p.sy
        ones_c = ctx.enter_context(nc.sbuf_tensor("onesc", [128, 4], F32))
        ones_r = ctx.enter_context(nc.sbuf_tensor("onesr", [1, 128], F32))
        ones_b = ctx.enter_context(nc.sbuf_tensor("onesb", [128, 1], BF16))
        idb = ctx.enter_context(nc.sbuf_tensor("idb", [128, 128], BF16))
        consts = {"ones_col": ones_c[:, 0:1], "ones_row": ones_r,
                  "eps": ones_c[:, 1:2], "one1": ones_c}
        gp.dma(lambda e: e.dma_start(out=ones_c[:, :], in_=onescol))
        gp.dma(lambda e: e.dma_start(out=ones_r[:, :], in_=onesrow))
        V_C = gp.dma(lambda e: e.dma_start(out=idb[:, :], in_=identb))
        ve.wait(gp, V_C)
        V_OB = ve.emit(lambda e: e.tensor_copy(ones_b[:, :], ones_c[:, 0:1]))

        ph1 = ctx.enter_context(contextlib.ExitStack())
        xhB = ph1.enter_context(nc.sbuf_tensor("xhB", [128, NCH * KL], BF16))
        ph1b = ph1.enter_context(contextlib.ExitStack())
        xsb = ph1b.enter_context(nc.sbuf_tensor("xsb", [128, NCH * TPC], F32))
        xhalo = ph1b.enter_context(nc.sbuf_tensor("xhalo", [128, NCH * 256],
                                                  F32))
        ssb = ph1b.enter_context(nc.sbuf_tensor("ssb", [128, NCH * 64], F32))
        sred = ph1b.enter_context(nc.sbuf_tensor("sred", [128, 8], F32))
        gp.wait(gp, gp.v)
        gp.dma(lambda e: e.dma_start(
            out=xsb.rearrange("p (cc t) -> p cc t", cc=NCH),
            in_=xT.rearrange("(cc p) t -> p cc t", p=128)))
        gp.dma(lambda e: e.dma_start(
            out=xhalo.rearrange("p (cc t) -> p cc t", cc=NCH),
            in_=haloT.rearrange("(cc p) t -> p cc t", p=128)))
        V_XIN = gp.dma(lambda e: e.dma_start(
            out=ssb.rearrange("p (cc t) -> p cc t", cc=NCH),
            in_=scat.rearrange("(cc p) t -> p cc t", p=128)))
        ve.wait(gp, V_XIN)
        V_SC = None
        for cc in range(NCH):
            ve.wait(ve, ve.v)
            ve.emit(lambda e, cc=cc: e.tensor_reduce(
                sred[:, :],
                ssb.rearrange("p (cc e q) -> p cc e q", cc=NCH, q=8)[:, cc],
                mybir.AxisListType.X, ALU.add))
            ve.wait(ve, ve.v)
            V_SC = ve.emit(lambda e, cc=cc: e.tensor_add(
                xsb[:, cc * TPC: cc * TPC + 8], sred[:, :],
                xsb[:, cc * TPC: cc * TPC + 8]))
        sy.wait(ve, V_SC)
        V_X2ST = sy.dma(lambda e: e.dma_start(
            out=dxx.rearrange("(cc p) t -> p cc t", p=128),
            in_=xsb.rearrange("p (cc t) -> p cc t", cc=NCH)))
        V_LN, _ = emit_ln(p, nc, xsb, TPC, consts, xhB,
                          [(ve, V_SC), (gp, V_XIN)], tag="bo",
                          alloc_ctx=ph1b, outW=KL, out_off=256)
        V_LNH, _ = emit_ln(p, nc, xhalo, 256, consts, xhB,
                           [(gp, V_XIN)], tag="bh",
                           alloc_ctx=ph1b, outW=KL, out_off=0)
        for eng in (te, sc, gp, sy):
            eng.wait(ve, V_LN)
            eng.wait(ve, V_LNH)
        sy.wait(sy, V_X2ST)
        for eng in (te, sc, gp, ve):
            eng.wait(sy, V_X2ST)
        ph1b.close()

        att_r = ctx.enter_context(contextlib.ExitStack())
        ySB = att_r.enter_context(nc.sbuf_tensor("ySB", [128, 16 * TPC], BF16,
                                                 side="right"))
        att_r2 = att_r.enter_context(contextlib.ExitStack())
        qT = att_r2.enter_context(nc.sbuf_tensor("qT", [128, 16 * TPC], BF16,
                                                 side="right"))
        kT = att_r2.enter_context(nc.sbuf_tensor("kT", [128, 16 * KL], BF16,
                                                 side="right"))
        vS = att_r2.enter_context(nc.sbuf_tensor("vS", [128, 160 * 128], BF16,
                                                 side="right"))
        msk = att_r2.enter_context(nc.sbuf_tensor("msk", [128, 16 * 256], BF16,
                                                  side="right"))
        gp.wait(gp, gp.v)
        V_MSK = gp.dma(lambda e: e.dma_start(out=msk[:, :], in_=maskB))

        ph2 = ctx.enter_context(contextlib.ExitStack())
        emit_ffcols(p, nc, ph2, xhB, KL, wqTp, qT, TPC, AF.Identity, NCH,
                    tag="bq", src_off=256, L=TPC)
        emit_ffcols(p, nc, ph2, xhB, KL, wkTp, kT, KL, AF.Identity, NCH,
                    tag="bk", L=KL)
        for h in range(16):
            hctx = ph2.enter_context(contextlib.ExitStack())
            vTh = hctx.enter_context(nc.sbuf_tensor(f"vTh{h}", [128, KL],
                                                    BF16))
            ps_vt = hctx.enter_context(nc.psum_tensor(f"pvt{h}", [128, 128],
                                                      BF16))
            emit_ffcols(p, nc, hctx, xhB, KL,
                        wvT[:, h * 128:(h + 1) * 128], vTh, KL,
                        AF.Identity, 1, tag=f"bv{h}", L=KL)
            te.wait(sc, sc.v)
            te.wait(ve, V_OB)
            for tc in range(10):
                te.wait(ve, ve.v)
                te.emit(lambda e, tc=tc, ps_vt=ps_vt, vTh=vTh: e.transpose(
                    ps_vt[:, :], vTh[:, tc * 128:(tc + 1) * 128], idb[:, :]))
                ve.wait(te, te.v)
                ve.emit(lambda e, h=h, tc=tc, ps_vt=ps_vt: e.tensor_copy(
                    vS[:, (tc * 16 + h) * 128:(tc * 16 + h) * 128 + 128],
                    ps_vt[:, :]))
            te.wait(ve, ve.v)
            hctx.close()
        V_VS = ve.v
        V_TEP2 = te.v
        ph2.close()
        ph1.close()

        ph3 = ctx.enter_context(contextlib.ExitStack())
        tq = ph3.enter_context(nc.sbuf_tensor("tq", [128, 2 * TPC], F32))
        tk = ph3.enter_context(nc.sbuf_tensor("tk", [128, 2 * KL], F32))
        scr = ph3.enter_context(nc.sbuf_tensor("scr", [64, 3 * KL], F32))
        gp.wait(gp, gp.v)
        gp.wait(te, V_TEP2)
        gp.wait(ve, V_VS)
        gp.wait(sc, sc.v)
        gp.dma(lambda e: e.dma_start(out=tq[:, :], in_=tabQ))
        V_TAB = gp.dma(lambda e: e.dma_start(out=tk[:, :], in_=tabK))
        ve.wait(gp, V_TAB)
        ve.wait(sc, sc.v)

        def rope(buf, tab, W):
            cos0 = tab[0:64, 0:W]
            cos1 = tab[64:128, 0:W]
            sin0 = tab[0:64, W:2 * W]
            sin1 = tab[64:128, W:2 * W]
            for h in range(16):
                qe = buf[:, h * W:(h + 1) * W][0:64, :]
                qo = buf[:, h * W:(h + 1) * W][64:128, :]
                s1 = scr[0:64, 0:W]
                s2 = scr[0:64, KL:KL + W]
                s3 = scr[0:64, 2 * KL:2 * KL + W]
                ve.wait(ve, ve.v)
                ve.emit(lambda e, qe=qe, c=cos0, s1=s1: e.tensor_mul(
                    s1, qe, c))
                ve.emit(lambda e, qo=qo, sn=sin1, s2=s2: e.tensor_mul(
                    s2, qo, sn))
                ve.emit(lambda e, qe=qe, sn=sin0, s3=s3: e.tensor_mul(
                    s3, qe, sn))
                ve.wait(ve, ve.v)
                ve.emit(lambda e, qe=qe, s1=s1, s2=s2: e.tensor_sub(
                    qe, s1, s2))
                ve.wait(ve, ve.v)
                ve.emit(lambda e, qo=qo, c=cos1, s2=s2: e.tensor_mul(
                    s2, qo, c))
                ve.wait(ve, ve.v)
                ve.emit(lambda e, qo=qo, s2=s2, s3=s3: e.tensor_add(
                    qo, s3, s2))
        rope(qT, tq, TPC)
        rope(kT, tk, KL)
        V_ROPE = ve.v
        ph3.close()

        ph4 = ctx.enter_context(contextlib.ExitStack())
        exB = ph4.enter_context(nc.sbuf_tensor("exB", [128, 4 * 256], BF16))
        rbc = ph4.enter_context(nc.sbuf_tensor("rbc", [128, 256], F32))
        rro = ph4.enter_context(nc.sbuf_tensor("rro", [1, 256], F32))
        ps_s = ph4.enter_context(nc.psum_tensor("psbs", [128, 256], F32))
        ps_d = ph4.enter_context(nc.psum_tensor("psbd", [1, 256], F32))
        ps_y = ph4.enter_context(nc.psum_tensor("psby", [128, 256], F32))
        ps_rb = ph4.enter_context(nc.psum_tensor("psrb", [128, 256], F32))
        te.wait(ve, V_ROPE)
        te.wait(gp, V_MSK)
        for ci in range(4):
            for h in range(16):
                for kc in range(4):
                    te.wait(ve, ve.v)
                    te.emit(lambda e, ci=ci, h=h, kc=kc: e.matmul(
                        ps_s[:, :],
                        kT[:, h * KL + ci * 256 + kc * 128:
                           h * KL + ci * 256 + (kc + 1) * 128],
                        qT[:, h * TPC + ci * 256: h * TPC + (ci + 1) * 256],
                        start=True, stop=True))
                    ve.wait(te, te.v)
                    ve.emit(lambda e, ci=ci, kc=kc: e.scalar_tensor_tensor(
                        exB[:, kc * 256:(kc + 1) * 256], ps_s[:, :], RS,
                        msk[:, (ci * 4 + kc) * 256:(ci * 4 + kc + 1) * 256],
                        ALU.mult, ALU.add))
                    sc.wait(ve, ve.v)
                    sc.emit(lambda e, kc=kc: e.activation(
                        exB[:, kc * 256:(kc + 1) * 256],
                        exB[:, kc * 256:(kc + 1) * 256], AF.Exp))
                te.wait(sc, sc.v)
                for kc in range(4):
                    te.emit(lambda e, kc=kc: e.matmul(
                        ps_d[:, :], ones_b[:, 0:1],
                        exB[:, kc * 256:(kc + 1) * 256],
                        start=(kc == 0), stop=(kc == 3)))
                for kc in range(4):
                    te.emit(lambda e, ci=ci, h=h, kc=kc: e.matmul(
                        ps_y[:, :],
                        vS[:, ((ci * 2 + kc) * 16 + h) * 128:
                           ((ci * 2 + kc) * 16 + h) * 128 + 128],
                        exB[:, kc * 256:(kc + 1) * 256],
                        start=(kc == 0), stop=(kc == 3)))
                V_YMM = te.v
                ve.wait(te, V_YMM)
                ve.emit(lambda e: e.reciprocal(rro[0:1, :], ps_d[:, :]))
                te.wait(ve, ve.v)
                V_RB = te.emit(lambda e: e.matmul(
                    ps_rb[:, :], ones_r[0:1, :], rro[0:1, :],
                    start=True, stop=True))
                ve.wait(te, V_RB)
                ve.emit(lambda e: e.tensor_copy(rbc[:, :], ps_rb[:, :]))
                ve.wait(ve, ve.v)
                ve.emit(lambda e, ci=ci, h=h: e.tensor_mul(
                    ySB[:, h * TPC + ci * 256: h * TPC + (ci + 1) * 256],
                    ps_y[:, :], rbc[:, :]))
        V_ATT = ve.v
        ph4.close()

        ph5a = ctx.enter_context(contextlib.ExitStack())
        atout = ph5a.enter_context(nc.sbuf_tensor("atout", [128, NCH * TPC],
                                                  BF16))
        for eng in (te, sc, gp):
            eng.wait(ve, V_ATT)
        emit_ffcols(p, nc, ph5a, ySB, TPC, woD, atout, TPC, AF.Identity, NCH,
                    tag="bw")
        V_WOS = sc.v
        sy.wait(sc, V_WOS)
        sy.wait(sy, sy.v)
        V_ATST = sy.dma(lambda e: e.dma_start(
            out=datt.rearrange("(cc p) t -> p cc t", p=128),
            in_=atout.rearrange("p (cc t) -> p cc t", cc=NCH)))
        ph5a.close()
        att_r2.close()
        att_r.close()

        ph5 = ctx.enter_context(contextlib.ExitStack())
        xsb2 = ph5.enter_context(nc.sbuf_tensor("xsb2", [128, NCH * TPC], F32))
        ph5c = ph5.enter_context(contextlib.ExitStack())
        attin = ph5c.enter_context(nc.sbuf_tensor("attin", [128, NCH * TPC],
                                                  BF16))
        gp.wait(gp, gp.v)
        gp.wait(sy, V_ATST)
        gp.wait(sc, sc.v)
        gp.wait(te, te.v)
        gp.dma(lambda e: e.dma_start(
            out=xsb2.rearrange("p (cc t) -> p cc t", cc=NCH),
            in_=dxx.rearrange("(cc p) t -> p cc t", p=128)))
        V_RL = gp.dma(lambda e: e.dma_start(
            out=attin.rearrange("p (cc t) -> p cc t", cc=NCH),
            in_=datt.rearrange("(cc p) t -> p cc t", p=128)))
        ve.wait(gp, V_RL)
        V_X3 = None
        for cc in range(NCH):
            ve.wait(ve, ve.v)
            V_X3 = ve.emit(lambda e, cc=cc: e.tensor_add(
                xsb2[:, cc * TPC:(cc + 1) * TPC],
                attin[:, cc * TPC:(cc + 1) * TPC],
                xsb2[:, cc * TPC:(cc + 1) * TPC]))
        ph5c.close()
        sy.wait(ve, V_X3)
        sy.dma(lambda e: e.dma_start(
            out=x3T.rearrange("(cc p) t -> p cc t", p=128),
            in_=xsb2.rearrange("p (cc t) -> p cc t", cc=NCH)))
        emit_ffn0(p, nc, ph5, xsb2, consts, w1T, w3T, w2T, wrT, part,
                  [(ve, V_X3)], tag="c")
        sy.wait(sy, sy.v)
        ph5.close()
        p.finish()
    return nc



def build_M1():
    nc = bass.Bass("TRN2", target_bir_lowering=False)
    xT = nc.dram_tensor("xT", [D, TPC], F32, kind="ExternalInput").ap()
    scat = nc.dram_tensor("scat", [D, 64], F32, kind="ExternalInput").ap()
    onescol = nc.dram_tensor("onescol", [128, 4], F32, kind="ExternalInput").ap()
    onesrow = nc.dram_tensor("onesrow", [1, 128], F32, kind="ExternalInput").ap()
    x4T = nc.dram_tensor("x4T", [D, TPC], F32, kind="ExternalOutput").ap()
    xhT = nc.dram_tensor("xhT", [D, TPC], F32, kind="ExternalOutput").ap()

    with contextlib.ExitStack() as ctx:
        p = Prog(nc, ctx)
        gp, te, sc, ve, sy = p.gp, p.te, p.sc, p.ve, p.sy
        ones_c = ctx.enter_context(nc.sbuf_tensor("onesc", [128, 4], F32))
        ones_r = ctx.enter_context(nc.sbuf_tensor("onesr", [1, 128], F32))
        consts = {"ones_col": ones_c[:, 0:1], "ones_row": ones_r,
                  "eps": ones_c[:, 1:2], "one1": ones_c}
        xsb = ctx.enter_context(nc.sbuf_tensor("xsb", [128, NCH * TPC], F32))
        xh = ctx.enter_context(nc.sbuf_tensor("xh", [128, NCH * TPC], F32))
        ssb = ctx.enter_context(nc.sbuf_tensor("ssb", [128, NCH * 64], F32))
        sred = ctx.enter_context(nc.sbuf_tensor("sred", [128, 8], F32))
        gp.dma(lambda e: e.dma_start(out=ones_c[:, :], in_=onescol))
        gp.dma(lambda e: e.dma_start(out=ones_r[:, :], in_=onesrow))
        gp.dma(lambda e: e.dma_start(
            out=xsb.rearrange("p (cc t) -> p cc t", cc=NCH),
            in_=xT.rearrange("(cc p) t -> p cc t", p=128)))
        V_XIN = gp.dma(lambda e: e.dma_start(
            out=ssb.rearrange("p (cc t) -> p cc t", cc=NCH),
            in_=scat.rearrange("(cc p) t -> p cc t", p=128)))
        ve.wait(gp, V_XIN)
        V_SC = None
        for cc in range(NCH):
            ve.wait(ve, ve.v)
            ve.emit(lambda e, cc=cc: e.tensor_reduce(
                sred[:, :],
                ssb.rearrange("p (cc e q) -> p cc e q", cc=NCH, q=8)[:, cc],
                mybir.AxisListType.X, ALU.add))
            ve.wait(ve, ve.v)
            V_SC = ve.emit(lambda e, cc=cc: e.tensor_add(
                xsb[:, cc * TPC: cc * TPC + 8], sred[:, :],
                xsb[:, cc * TPC: cc * TPC + 8]))
        sy.wait(ve, V_SC)
        sy.dma(lambda e: e.dma_start(
            out=x4T.rearrange("(cc p) t -> p cc t", p=128),
            in_=xsb.rearrange("p (cc t) -> p cc t", cc=NCH)))
        V_LN, _ = emit_ln(p, nc, xsb, TPC, consts, xh,
                          [(ve, V_SC), (gp, V_XIN)], tag="m", alloc_ctx=ctx)
        sy.wait(ve, V_LN)
        sy.dma(lambda e: e.dma_start(
            out=xhT.rearrange("(cc p) t -> p cc t", p=128),
            in_=xh.rearrange("p (cc t) -> p cc t", cc=NCH)))
        sy.wait(sy, sy.v)
        p.finish()
    return nc



def build_M2():
    nc = bass.Bass("TRN2", target_bir_lowering=False)
    xh = nc.dram_tensor("xh", [D, T], F32, kind="ExternalInput").ap()
    cw = nc.dram_tensor("cw", [D, 4], F32, kind="ExternalInput").ap()
    cb = nc.dram_tensor("cb", [128, NCH], F32, kind="ExternalInput").ap()
    xpT = nc.dram_tensor("xpT", [D, 96], F32, kind="ExternalInput").ap()
    dtwT = nc.dram_tensor("dtwT", [64, TPC], F32, kind="ExternalInput").ap()
    dtb = nc.dram_tensor("dtb", [128, 8], F32, kind="ExternalInput").ap()
    Ain = nc.dram_tensor("Ain", [128, 128], F32, kind="ExternalInput").ap()
    Dvin = nc.dram_tensor("Dvin", [128, 8], F32, kind="ExternalInput").ap()
    onescol = nc.dram_tensor("onescol", [128, 4], F32, kind="ExternalInput").ap()
    onesrow = nc.dram_tensor("onesrow", [1, 128], F32, kind="ExternalInput").ap()
    yout = nc.dram_tensor("yout", [TPC, T], F32, kind="ExternalOutput").ap()

    TH = 1024
    with contextlib.ExitStack() as ctx:
        p = Prog(nc, ctx)
        gp, te, sc, ve, sy = p.gp, p.te, p.sc, p.ve, p.sy
        ones_c = ctx.enter_context(nc.sbuf_tensor("onesc", [128, 4], F32))
        ones_r = ctx.enter_context(nc.sbuf_tensor("onesr", [1, 128], F32))
        bst = ctx.enter_context(nc.sbuf_tensor("bst", [1, 2 * TH], F32))
        cwsb = ctx.enter_context(nc.sbuf_tensor("cwsb", [128, NCH * 4], F32))
        cbsb = ctx.enter_context(nc.sbuf_tensor("cbsb", [128, NCH], F32))
        xps = ctx.enter_context(nc.sbuf_tensor("xps", [128, NCH * 96], BF16))
        dtw = ctx.enter_context(nc.sbuf_tensor("dtw", [64, TPC], BF16))
        dtbs = ctx.enter_context(nc.sbuf_tensor("dtbs", [128, 8], F32))
        Acol = ctx.enter_context(nc.sbuf_tensor("Acol", [128, 8 * 16], F32))
        Dcol = ctx.enter_context(nc.sbuf_tensor("Dcol", [128, 8], F32))
        carry = ctx.enter_context(nc.sbuf_tensor("carry", [128, 128], F32))
        xcN = ctx.enter_context(nc.sbuf_tensor("xcN", [128, 8 * TH], F32))
        dltN = ctx.enter_context(nc.sbuf_tensor("dltN", [128, 8 * TH], F32))
        duN = ctx.enter_context(nc.sbuf_tensor("duN", [128, 8 * TH], F32))
        yac = ctx.enter_context(nc.sbuf_tensor("yac", [128, 8 * TH], F32))
        Bbc = ctx.enter_context(nc.sbuf_tensor("Bbc", [128, TH], F32))
        Cbc = ctx.enter_context(nc.sbuf_tensor("Cbc", [128, TH], F32))
        htl = ctx.enter_context(nc.sbuf_tensor("htl", [128, TH], F32))
        ttl = ctx.enter_context(nc.sbuf_tensor("ttl", [128, TH], F32))
        atl = ctx.enter_context(nc.sbuf_tensor("atl", [128, TH], F32))
        xdb = ctx.enter_context(nc.sbuf_tensor("xdb", [96, TH], F32))
        xdbb = ctx.enter_context(nc.sbuf_tensor("xdbb", [64, TH], BF16))
        pad = ctx.enter_context(nc.sbuf_tensor("pad", [128, TH + 3], F32))
        xct = ctx.enter_context(nc.sbuf_tensor("xct", [128, TH], F32))
        xcb = ctx.enter_context(nc.sbuf_tensor("xcb", [128, TH], BF16))
        ps_xd = [ctx.enter_context(nc.psum_tensor(f"pxd{i}", [96, 512], F32))
                 for i in range(2)]
        ps_dl = [ctx.enter_context(nc.psum_tensor(f"pdl{i}", [128, 512], F32))
                 for i in range(2)]
        ps_bc = [ctx.enter_context(nc.psum_tensor(f"pbc{i}", [128, 512], F32))
                 for i in range(4)]
        spl = ctx.enter_context(nc.sbuf_tensor("spl", [128, 2 * 512], F32))

        gp.dma(lambda e: e.dma_start(out=ones_c[:, :], in_=onescol))
        gp.dma(lambda e: e.dma_start(out=ones_r[:, :], in_=onesrow))
        gp.dma(lambda e: e.dma_start(
            out=cwsb.rearrange("p (cc k) -> p cc k", k=4),
            in_=cw.rearrange("(cc p) k -> p cc k", p=128)))
        gp.dma(lambda e: e.dma_start(out=cbsb[:, :], in_=cb))
        gp.dma(lambda e: e.dma_start(
            out=xps.rearrange("p (cc f) -> p cc f", f=96),
            in_=xpT.rearrange("(cc p) f -> p cc f", p=128)))
        gp.dma(lambda e: e.dma_start(out=dtw[:, :], in_=dtwT))
        gp.dma(lambda e: e.dma_start(out=dtbs[:, :], in_=dtb))
        gp.dma(lambda e: e.dma_start(out=Acol[:, :], in_=Ain))
        V_W = gp.dma(lambda e: e.dma_start(out=Dcol[:, :], in_=Dvin))
        ve.wait(gp, V_W)
        V_CZ = ve.emit(lambda e: e.memset(carry[:, :], 0.0))
        sc.wait(gp, V_W)
        V_AE = sc.emit(lambda e: e.activation(Acol[:, :], Acol[:, :], AF.Exp))
        ve.wait(sc, V_AE)
        V_AN = ve.emit(lambda e: e.tensor_scalar(Acol[:, :], Acol[:, :], -1.0,
                                                 None, ALU.mult))
        sc.wait(ve, V_AN)

        for th in range(2):
            GP_TH = gp.v
            for cc in range(NCH):
                tgt = xcN[:, cc * TH:(cc + 1) * TH] if cc < 8 else xct[:, :]
                gp.wait(ve, ve.v)
                gp.wait(sc, sc.v)
                gp.wait(gp, gp.v)
                if th == 0:
                    ve.wait(gp, gp.v)
                    ve.emit(lambda e: e.memset(pad[:, 0:3], 0.0))
                    V_PD = gp.dma(lambda e, cc=cc: e.dma_start(
                        out=pad[:, 3:3 + TH],
                        in_=xh[cc * 128:(cc + 1) * 128, 0:TH]))
                else:
                    V_PD = gp.dma(lambda e, cc=cc: e.dma_start(
                        out=pad[:, 0:3 + TH],
                        in_=xh[cc * 128:(cc + 1) * 128, TH - 3:T]))
                ve.wait(gp, V_PD)
                ve.wait(ve, ve.v)
                ve.wait(sc, sc.v)
                ve.wait(te, te.v)
                ve.emit(lambda e, cc=cc: e.tensor_scalar(
                    ttl[:, :], pad[:, 0:TH], cwsb[:, 4 * cc:4 * cc + 1], None,
                    ALU.mult))
                for k in range(1, 4):
                    ve.wait(ve, ve.v)
                    ve.emit(lambda e, cc=cc, k=k: e.scalar_tensor_tensor(
                        ttl[:, :], pad[:, k:k + TH],
                        cwsb[:, 4 * cc + k:4 * cc + k + 1],
                        ttl[:, :], ALU.mult, ALU.add))
                sc.wait(ve, ve.v)
                sc.wait(te, te.v)
                V_SIL = sc.emit(lambda e, cc=cc, tgt=tgt: e.activation(
                    tgt, ttl[:, :], AF.Silu, bias=cbsb[:, cc:cc + 1]))
                sc.wait(sc, V_SIL)
                V_XCB = sc.emit(lambda e, cc=cc, tgt=tgt: e.activation(
                    xcb[:, :], tgt, AF.Identity))
                te.wait(sc, V_XCB)
                for w2 in range(2):
                    te.emit(lambda e, cc=cc, w2=w2: e.matmul(
                        ps_xd[w2][:, :], xps[:, cc * 96:(cc + 1) * 96],
                        xcb[:, w2 * 512:(w2 + 1) * 512],
                        start=(cc == 0), stop=(cc == NCH - 1)))
            V_XD = te.v
            ve.wait(te, V_XD)
            for w2 in range(2):
                ve.wait(ve, ve.v)
                ve.emit(lambda e, w2=w2: e.tensor_copy(
                    xdb[:, w2 * 512:(w2 + 1) * 512], ps_xd[w2][:, :]))
            V_XDB = ve.v
            ve.wait(ve, V_XDB)
            V_XBB = ve.emit(lambda e: e.tensor_copy(xdbb[:, :], xdb[0:64, :]))
            te.wait(ve, V_XBB)
            for ec in range(8):
                if ec >= 1:
                    te.wait(sc, sc.v)
                for w2 in range(2):
                    te.emit(lambda e, ec=ec, w2=w2: e.matmul(
                        ps_dl[w2][:, :], dtw[:, ec * 128:(ec + 1) * 128],
                        xdbb[:, w2 * 512:(w2 + 1) * 512],
                        start=True, stop=True))
                sc.wait(te, te.v)
                for w2 in range(2):
                    sc.wait(sc, sc.v)
                    sc.wait(ve, ve.v)
                    sc.emit(lambda e, ec=ec, w2=w2: e.activation(
                        spl[:, 0:512], ps_dl[w2][:, :], AF.Abs,
                        bias=dtbs[:, ec:ec + 1]))
                    sc.wait(sc, sc.v)
                    sc.emit(lambda e: e.activation(
                        spl[:, 0:512], spl[:, 0:512], AF.Exp, scale=-1.0))
                    sc.wait(sc, sc.v)
                    sc.emit(lambda e: e.activation(
                        spl[:, 0:512], spl[:, 0:512], AF.Ln,
                        bias=ones_c[:, 0:1]))
                    sc.wait(sc, sc.v)
                    sc.emit(lambda e, ec=ec, w2=w2: e.activation(
                        spl[:, 512:1024], ps_dl[w2][:, :], AF.Relu,
                        bias=dtbs[:, ec:ec + 1]))
                    ve.wait(sc, sc.v)
                    ve.wait(ve, ve.v)
                    ve.emit(lambda e, ec=ec, w2=w2: e.tensor_add(
                        dltN[:, ec * TH + w2 * 512: ec * TH + (w2 + 1) * 512],
                        spl[:, 0:512], spl[:, 512:1024]))
            V_DLT = ve.v
            for ec in range(8):
                ve.wait(ve, ve.v)
                ve.emit(lambda e, ec=ec: e.tensor_mul(
                    duN[:, ec * TH:(ec + 1) * TH],
                    dltN[:, ec * TH:(ec + 1) * TH],
                    xcN[:, ec * TH:(ec + 1) * TH]))
            V_DU = ve.v

            for n in range(16):
                gp.wait(gp, gp.v)
                gp.wait(te, te.v)
                gp.wait(ve, V_XDB)
                gp.dma(lambda e, n=n: e.dma_start(
                    out=bst[0:1, 0:TH], in_=xdb[64 + n:65 + n, 0:TH]))
                V_BST = gp.dma(lambda e, n=n: e.dma_start(
                    out=bst[0:1, TH:2 * TH], in_=xdb[80 + n:81 + n, 0:TH]))
                te.wait(ve, ve.v)
                te.wait(gp, V_BST)
                for w2 in range(2):
                    te.emit(lambda e, w2=w2: e.matmul(
                        ps_bc[w2][:, :], ones_r[0:1, :],
                        bst[0:1, w2 * 512:(w2 + 1) * 512],
                        start=True, stop=True))
                    te.emit(lambda e, w2=w2: e.matmul(
                        ps_bc[2 + w2][:, :], ones_r[0:1, :],
                        bst[0:1, TH + w2 * 512: TH + (w2 + 1) * 512],
                        start=True, stop=True))
                ve.wait(te, te.v)
                for w2 in range(2):
                    ve.wait(ve, ve.v)
                    ve.emit(lambda e, w2=w2: e.tensor_copy(
                        Bbc[:, w2 * 512:(w2 + 1) * 512], ps_bc[w2][:, :]))
                    ve.wait(ve, ve.v)
                    ve.emit(lambda e, w2=w2: e.tensor_copy(
                        Cbc[:, w2 * 512:(w2 + 1) * 512], ps_bc[2 + w2][:, :]))
                V_BCC = ve.v
                for ec in range(8):
                    idx = ec * 16 + n
                    sc.wait(ve, ve.v)
                    sc.wait(sc, sc.v)
                    V_A = sc.emit(lambda e, ec=ec, n=n: e.activation(
                        atl[:, :], dltN[:, ec * TH:(ec + 1) * TH], AF.Exp,
                        scale=Acol[:, ec * 16 + n: ec * 16 + n + 1]))
                    ve.wait(sc, V_A)
                    ve.wait(ve, ve.v)
                    ve.emit(lambda e, ec=ec: e.tensor_mul(
                        ttl[:, :], duN[:, ec * TH:(ec + 1) * TH], Bbc[:, :]))
                    ve.wait(ve, ve.v)
                    ve.emit(lambda e, idx=idx: e.tensor_tensor_scan(
                        htl[:, :], atl[:, :], ttl[:, :],
                        carry[:, idx:idx + 1], ALU.mult, ALU.add))
                    ve.wait(ve, ve.v)
                    ve.emit(lambda e, idx=idx: e.tensor_copy(
                        carry[:, idx:idx + 1], htl[:, TH - 1:TH]))
                    ve.wait(ve, ve.v)
                    ve.emit(lambda e: e.tensor_mul(ttl[:, :], htl[:, :],
                                                   Cbc[:, :]))
                    ve.wait(ve, ve.v)
                    if n == 0:
                        ve.emit(lambda e, ec=ec: e.tensor_copy(
                            yac[:, ec * TH:(ec + 1) * TH], ttl[:, :]))
                    else:
                        ve.emit(lambda e, ec=ec: e.tensor_add(
                            yac[:, ec * TH:(ec + 1) * TH], ttl[:, :],
                            yac[:, ec * TH:(ec + 1) * TH]))
            for ec in range(8):
                ve.wait(ve, ve.v)
                ve.emit(lambda e, ec=ec: e.scalar_tensor_tensor(
                    yac[:, ec * TH:(ec + 1) * TH],
                    xcN[:, ec * TH:(ec + 1) * TH], Dcol[:, ec:ec + 1],
                    yac[:, ec * TH:(ec + 1) * TH], ALU.mult, ALU.add))
            V_Y = ve.v
            sy.wait(ve, V_Y)
            sy.wait(sy, sy.v)
            sy.dma(lambda e, th=th: e.dma_start(
                out=yout.rearrange("(ec p) t -> p ec t", p=128)
                    [:, :, th * TH:(th + 1) * TH],
                in_=yac.rearrange("p (ec t) -> p ec t", t=TH)))
            ve.wait(sy, sy.v)
        sy.wait(sy, sy.v)
        p.finish()
    return nc



def emit_ffn1(p, nc, ctx, xsb, consts, wr16T, sgD, suD, sdD, gwD, uwD, dwD,
              deps, tag="M"):
    gp, te, sc, ve, sy = p.gp, p.te, p.sc, p.ve, p.sy
    E = 16
    fctx = ctx.enter_context(contextlib.ExitStack())
    wn = fctx.enter_context(nc.sbuf_tensor(f"wn{tag}", [128, 8 * E], BF16))
    wTr = fctx.enter_context(nc.sbuf_tensor(f"wTr{tag}", [16, TPC], BF16))
    wrow = fctx.enter_context(nc.sbuf_tensor(f"wrw{tag}", [1, TPC], BF16))
    onebr = fctx.enter_context(nc.sbuf_tensor(f"obr{tag}", [1, 128], BF16))
    wbc = fctx.enter_context(nc.sbuf_tensor(f"wbc{tag}", [128, TPC], F32))
    xh = fctx.enter_context(nc.sbuf_tensor(f"xh{tag}", [128, NCH * TPC], BF16))
    V_LN, stats = emit_ln(p, nc, xsb, TPC, consts, xh, deps, tag=tag,
                          alloc_ctx=fctx)
    for eng in (te, sc, gp):
        eng.wait(ve, V_LN)
    V_RT = emit_router(p, nc, fctx, xsb, stats, consts, E, wr16T, wn, tag=tag)
    rctx = fctx.enter_context(contextlib.ExitStack())
    ps_tw = rctx.enter_context(nc.psum_tensor(f"ptw{tag}", [16, 128], BF16))
    te.wait(ve, V_RT)
    ve.wait(ve, V_RT)
    V_OBR = ve.emit(lambda e: e.tensor_copy(onebr[0:1, :],
                                            consts["ones_row"][0:1, :]))
    for tc in range(8):
        te.wait(ve, ve.v)
        te.emit(lambda e, tc=tc: e.transpose(
            ps_tw[:, :], wn[:, tc * E:(tc + 1) * E], consts["idb"][:, :]))
        ve.wait(te, te.v)
        ve.emit(lambda e, tc=tc: e.tensor_copy(
            wTr[:, tc * 128:(tc + 1) * 128], ps_tw[:, :]))
    V_WTR = ve.v
    rctx.close()

    sctx = fctx.enter_context(contextlib.ExitStack())
    gsh = sctx.enter_context(nc.sbuf_tensor(f"gsh{tag}", [128, NCH * TPC],
                                            BF16))
    emit_ffcols(p, nc, sctx, xh, TPC, sgD, gsh, TPC, AF.Silu, NCH,
                tag=tag + "g")
    te.wait(sc, sc.v)
    emit_ffcols(p, nc, sctx, xh, TPC, suD, None, TPC, None, NCH,
                tag=tag + "u", addto=gsh, addW=TPC, multo=True)
    V_SH = ve.v
    for eng in (te, gp):
        eng.wait(ve, V_SH)
    emit_ffcols(p, nc, sctx, gsh, TPC, sdD, None, TPC, None, NCH,
                tag=tag + "d", addto=xsb, addW=TPC)
    V_SD = ve.v
    sctx.close()

    ectx = fctx.enter_context(contextlib.ExitStack())
    gE = ectx.enter_context(nc.sbuf_tensor(f"gE{tag}", [128, 8 * TPC], BF16))
    ps_wb = ectx.enter_context(nc.psum_tensor(f"pwb{tag}", [128, 512], F32))
    for e_i in range(16):
        gp.wait(gp, gp.v)
        gp.wait(te, te.v)
        gp.wait(ve, V_WTR)
        V_WRD = gp.dma(lambda e, e_i=e_i: e.dma_start(
            out=wrow[0:1, :], in_=wTr[e_i:e_i + 1, :]))
        for w2 in range(2):
            te.wait(gp, V_WRD)
            te.wait(ve, max(V_OBR, ve.v))
            te.emit(lambda e, w2=w2: e.matmul(
                ps_wb[:, :], onebr[0:1, :],
                wrow[0:1, w2 * 512:(w2 + 1) * 512], start=True, stop=True))
            ve.wait(te, te.v)
            ve.emit(lambda e, w2=w2: e.tensor_copy(
                wbc[:, w2 * 512:(w2 + 1) * 512], ps_wb[:, :]))
        V_WBC = ve.v
        emit_ffcols(p, nc, ectx, xh, TPC,
                    gwD[e_i * D:(e_i + 1) * D, :], gE, TPC, AF.Silu, 8,
                    tag=f"{tag}G{e_i}")
        te.wait(sc, sc.v)
        emit_ffcols(p, nc, ectx, xh, TPC,
                    uwD[e_i * D:(e_i + 1) * D, :], None, TPC, None, 8,
                    tag=f"{tag}U{e_i}", addto=gE, addW=TPC, multo=True)
        ve.wait(ve, ve.v)
        for fc in range(8):
            ve.wait(ve, ve.v)
            ve.emit(lambda e, fc=fc: e.tensor_mul(
                gE[:, fc * TPC:(fc + 1) * TPC],
                gE[:, fc * TPC:(fc + 1) * TPC], wbc[:, :]))
        V_HS = ve.v
        for eng in (te, gp):
            eng.wait(ve, V_HS)
        emit_ffcols(p, nc, ectx, gE, TPC,
                    dwD[e_i * TPC:(e_i + 1) * TPC, :], None, TPC, None, NCH,
                    ncc=8, tag=f"{tag}D{e_i}", addto=xsb, addW=TPC)
    V_OUT = ve.v
    ectx.close()
    fctx.close()
    return V_OUT



def build_F2():
    nc = bass.Bass("TRN2", target_bir_lowering=False)
    xT = nc.dram_tensor("xT", [D, TPC], F32, kind="ExternalInput").ap()
    yT = nc.dram_tensor("yT", [D, TPC], F32, kind="ExternalInput").ap()
    wr16T = nc.dram_tensor("wr16T", [D, 16], F32, kind="ExternalInput").ap()
    sgD = nc.dram_tensor("sgD", [D, D], F32, kind="ExternalInput").ap()
    suD = nc.dram_tensor("suD", [D, D], F32, kind="ExternalInput").ap()
    sdD = nc.dram_tensor("sdD", [D, D], F32, kind="ExternalInput").ap()
    gwD = nc.dram_tensor("gwD", [16 * D, TPC], F32, kind="ExternalInput").ap()
    uwD = nc.dram_tensor("uwD", [16 * D, TPC], F32, kind="ExternalInput").ap()
    dwD = nc.dram_tensor("dwD", [16 * TPC, D], F32, kind="ExternalInput").ap()
    wqD = nc.dram_tensor("wqD", [D, D], F32, kind="ExternalInput").ap()
    wkD = nc.dram_tensor("wkD", [D, 128], F32, kind="ExternalInput").ap()
    wvD = nc.dram_tensor("wvD", [D, 128], F32, kind="ExternalInput").ap()
    identb = nc.dram_tensor("identb", [128, 128], F32, kind="ExternalInput").ap()
    onescol = nc.dram_tensor("onescol", [128, 4], F32, kind="ExternalInput").ap()
    onesrow = nc.dram_tensor("onesrow", [1, 128], F32, kind="ExternalInput").ap()
    x6T = nc.dram_tensor("x6T", [D, TPC], F32, kind="ExternalOutput").ap()
    qTo = nc.dram_tensor("qTo", [128, 16 * TPC], BF16, kind="ExternalOutput").ap()
    kTo = nc.dram_tensor("kTo", [128, TPC], BF16, kind="ExternalOutput").ap()
    vO = nc.dram_tensor("vO", [TPC, 128], BF16, kind="ExternalOutput").ap()

    with contextlib.ExitStack() as ctx:
        p = Prog(nc, ctx)
        gp, te, sc, ve, sy = p.gp, p.te, p.sc, p.ve, p.sy
        ones_c = ctx.enter_context(nc.sbuf_tensor("onesc", [128, 4], F32))
        ones_r = ctx.enter_context(nc.sbuf_tensor("onesr", [1, 128], F32))
        idb = ctx.enter_context(nc.sbuf_tensor("idb", [128, 128], BF16))
        consts = {"ones_col": ones_c[:, 0:1], "ones_row": ones_r,
                  "eps": ones_c[:, 1:2], "one1": ones_c, "idb": idb}
        xsb = ctx.enter_context(nc.sbuf_tensor("xsb", [128, NCH * TPC], F32))
        gp.dma(lambda e: e.dma_start(out=ones_c[:, :], in_=onescol))
        gp.dma(lambda e: e.dma_start(out=ones_r[:, :], in_=onesrow))
        gp.dma(lambda e: e.dma_start(out=idb[:, :], in_=identb))
        gp.dma(lambda e: e.dma_start(
            out=xsb.rearrange("p (cc t) -> p cc t", cc=NCH),
            in_=xT.rearrange("(cc p) t -> p cc t", p=128)))
        ph0 = ctx.enter_context(contextlib.ExitStack())
        ysb = ph0.enter_context(nc.sbuf_tensor("ysb", [128, NCH * TPC], F32))
        V_IN = gp.dma(lambda e: e.dma_start(
            out=ysb.rearrange("p (cc t) -> p cc t", cc=NCH),
            in_=yT.rearrange("(cc p) t -> p cc t", p=128)))
        ve.wait(gp, V_IN)
        V_X5 = None
        for cc in range(NCH):
            ve.wait(ve, ve.v)
            V_X5 = ve.emit(lambda e, cc=cc: e.tensor_add(
                xsb[:, cc * TPC:(cc + 1) * TPC],
                ysb[:, cc * TPC:(cc + 1) * TPC],
                xsb[:, cc * TPC:(cc + 1) * TPC]))
        ph0.close()

        V_X6 = emit_ffn1(p, nc, ctx, xsb, consts, wr16T, sgD, suD, sdD,
                         gwD, uwD, dwD, [(ve, V_X5), (gp, V_IN)], tag="M")
        sy.wait(ve, V_X6)
        sy.dma(lambda e: e.dma_start(
            out=x6T.rearrange("(cc p) t -> p cc t", p=128),
            in_=xsb.rearrange("p (cc t) -> p cc t", cc=NCH)))

        ph2 = ctx.enter_context(contextlib.ExitStack())
        xh2 = ph2.enter_context(nc.sbuf_tensor("xh2", [128, NCH * TPC], BF16))
        V_LN2, _ = emit_ln(p, nc, xsb, TPC, consts, xh2, [(ve, V_X6)],
                           tag="q", alloc_ctx=ph2)
        for eng in (te, sc, gp):
            eng.wait(ve, V_LN2)
        qTt = ph2.enter_context(nc.sbuf_tensor("qTt", [128, NCH * TPC], BF16))
        emit_ffcols(p, nc, ph2, xh2, TPC, wqD, qTt, TPC, AF.Identity, NCH,
                    tag="q2")
        V_QS = sc.v
        kTt = ph2.enter_context(nc.sbuf_tensor("kTt", [128, TPC], BF16))
        emit_ffcols(p, nc, ph2, xh2, TPC, wkD, kTt, TPC, AF.Identity, 1,
                    tag="k2")
        V_KS = sc.v
        wvs = ph2.enter_context(nc.sbuf_tensor("wvs", [128, NCH * 128], BF16))
        vOt = ph2.enter_context(nc.sbuf_tensor("vOt", [128, 8 * 128], BF16))
        ps_vv = [ph2.enter_context(nc.psum_tensor(f"pvv{i}", [128, 128], F32))
                 for i in range(2)]
        gp.wait(gp, gp.v)
        gp.wait(te, te.v)
        gp.wait(sc, sc.v)
        V_WV = gp.dma(lambda e: e.dma_start(
            out=wvs.rearrange("p (cc f) -> p cc f", f=128),
            in_=wvD.rearrange("(cc p) f -> p cc f", p=128)))
        te.wait(gp, V_WV)
        te.wait(sc, V_KS)
        for tc in range(8):
            if tc >= 2:
                te.wait(sc, sc.v)
            for cc in range(NCH):
                te.emit(lambda e, tc=tc, cc=cc: e.matmul(
                    ps_vv[tc % 2][:, :],
                    xh2[:, cc * TPC + tc * 128: cc * TPC + (tc + 1) * 128],
                    wvs[:, cc * 128:(cc + 1) * 128],
                    start=(cc == 0), stop=(cc == NCH - 1)))
            sc.wait(te, te.v)
            sc.emit(lambda e, tc=tc: e.activation(
                vOt[:, tc * 128:(tc + 1) * 128], ps_vv[tc % 2][:, :],
                AF.Identity))
        V_VO = sc.v
        sy.wait(sc, V_QS)
        sy.wait(sy, sy.v)
        sy.dma(lambda e: e.dma_start(out=qTo, in_=qTt[:, :]))
        sy.wait(sc, V_KS)
        sy.wait(sy, sy.v)
        sy.dma(lambda e: e.dma_start(out=kTo, in_=kTt[:, :]))
        sy.wait(sc, V_VO)
        sy.wait(sy, sy.v)
        sy.dma(lambda e: e.dma_start(
            out=vO.rearrange("(tc p) d -> p tc d", p=128),
            in_=vOt.rearrange("p (tc d) -> p tc d", d=128)))
        sy.wait(sy, sy.v)
        ph2.close()
        p.finish()
    return nc



def build_Z():
    nc = bass.Bass("TRN2", target_bir_lowering=False)
    xT = nc.dram_tensor("xT", [D, TPC], F32, kind="ExternalInput").ap()
    qTd = nc.dram_tensor("qTd", [128, 16 * TPC], BF16, kind="ExternalInput").ap()
    kTf = nc.dram_tensor("kTf", [128, T], BF16, kind="ExternalInput").ap()
    vf = nc.dram_tensor("vf", [T, 128], BF16, kind="ExternalInput").ap()
    woD2 = nc.dram_tensor("woD2", [D, D], F32, kind="ExternalInput").ap()
    wr16T = nc.dram_tensor("wr16T", [D, 16], F32, kind="ExternalInput").ap()
    sgD = nc.dram_tensor("sgD", [D, D], F32, kind="ExternalInput").ap()
    suD = nc.dram_tensor("suD", [D, D], F32, kind="ExternalInput").ap()
    sdD = nc.dram_tensor("sdD", [D, D], F32, kind="ExternalInput").ap()
    gwD = nc.dram_tensor("gwD", [16 * D, TPC], F32, kind="ExternalInput").ap()
    uwD = nc.dram_tensor("uwD", [16 * D, TPC], F32, kind="ExternalInput").ap()
    dwD = nc.dram_tensor("dwD", [16 * TPC, D], F32, kind="ExternalInput").ap()
    identb = nc.dram_tensor("identb", [128, 128], F32, kind="ExternalInput").ap()
    onescol = nc.dram_tensor("onescol", [128, 4], F32, kind="ExternalInput").ap()
    onesrow = nc.dram_tensor("onesrow", [1, 128], F32, kind="ExternalInput").ap()
    x8T = nc.dram_tensor("x8T", [D, TPC], F32, kind="ExternalOutput").ap()

    with contextlib.ExitStack() as ctx:
        p = Prog(nc, ctx)
        gp, te, sc, ve, sy = p.gp, p.te, p.sc, p.ve, p.sy
        ones_c = ctx.enter_context(nc.sbuf_tensor("onesc", [128, 4], F32))
        ones_r = ctx.enter_context(nc.sbuf_tensor("onesr", [1, 128], F32))
        ones_b = ctx.enter_context(nc.sbuf_tensor("onesb", [128, 1], BF16))
        idb = ctx.enter_context(nc.sbuf_tensor("idb", [128, 128], BF16))
        consts = {"ones_col": ones_c[:, 0:1], "ones_row": ones_r,
                  "eps": ones_c[:, 1:2], "one1": ones_c, "idb": idb}
        xsb = ctx.enter_context(nc.sbuf_tensor("xsb", [128, NCH * TPC], F32))
        gp.dma(lambda e: e.dma_start(out=ones_c[:, :], in_=onescol))
        gp.dma(lambda e: e.dma_start(out=ones_r[:, :], in_=onesrow))
        gp.dma(lambda e: e.dma_start(out=idb[:, :], in_=identb))
        V_C = gp.dma(lambda e: e.dma_start(
            out=xsb.rearrange("p (cc t) -> p cc t", cc=NCH),
            in_=xT.rearrange("(cc p) t -> p cc t", p=128)))
        ve.wait(gp, V_C)
        V_OB = ve.emit(lambda e: e.tensor_copy(ones_b[:, :], ones_c[:, 0:1]))

        pha = ctx.enter_context(contextlib.ExitStack())
        ySB = pha.enter_context(nc.sbuf_tensor("ySB", [128, NCH * TPC], BF16))
        pha2 = pha.enter_context(contextlib.ExitStack())
        qTs = pha2.enter_context(nc.sbuf_tensor("qTs", [128, 16 * TPC], BF16))
        kTs = pha2.enter_context(nc.sbuf_tensor("kTs", [128, T], BF16))
        vSf = pha2.enter_context(nc.sbuf_tensor("vSf", [128, 16 * 128], BF16))
        exQ = pha2.enter_context(nc.sbuf_tensor("exQ", [128, 16 * 512], BF16))
        rbc = pha2.enter_context(nc.sbuf_tensor("rbc", [128, 512], F32))
        rro = pha2.enter_context(nc.sbuf_tensor("rro", [1, 512], F32))
        ps_s = pha2.enter_context(nc.psum_tensor("pzs", [128, 512], F32))
        ps_d = pha2.enter_context(nc.psum_tensor("pzd", [1, 512], F32))
        ps_y = pha2.enter_context(nc.psum_tensor("pzy", [128, 512], F32))
        ps_rb = pha2.enter_context(nc.psum_tensor("pzrb", [128, 512], F32))
        gp.wait(gp, gp.v)
        gp.dma(lambda e: e.dma_start(out=qTs[:, :], in_=qTd))
        gp.dma(lambda e: e.dma_start(out=kTs[:, :], in_=kTf))
        V_KV = gp.dma(lambda e: e.dma_start(
            out=vSf.rearrange("p (g d) -> p g d", d=128),
            in_=vf.rearrange("(g p) d -> p g d", p=128)))
        te.wait(gp, V_KV)
        te.wait(ve, V_OB)
        for h in range(16):
            for qh in range(2):
                for kc in range(16):
                    te.wait(sc, sc.v)
                    te.emit(lambda e, h=h, qh=qh, kc=kc: e.matmul(
                        ps_s[:, :], kTs[:, kc * 128:(kc + 1) * 128],
                        qTs[:, h * TPC + qh * 512: h * TPC + qh * 512 + 512],
                        start=True, stop=True))
                    sc.wait(te, te.v)
                    if kc == 0:
                        sc.wait(ve, ve.v)
                    sc.emit(lambda e, kc=kc: e.activation(
                        exQ[:, kc * 512:(kc + 1) * 512], ps_s[:, :], AF.Exp,
                        scale=RS, bias=ones_c[:, 2:3]))
                te.wait(sc, sc.v)
                for kc in range(16):
                    te.emit(lambda e, kc=kc: e.matmul(
                        ps_d[:, :], ones_b[:, 0:1],
                        exQ[:, kc * 512:(kc + 1) * 512],
                        start=(kc == 0), stop=(kc == 15)))
                for kc in range(16):
                    te.emit(lambda e, kc=kc: e.matmul(
                        ps_y[:, :], vSf[:, kc * 128:(kc + 1) * 128],
                        exQ[:, kc * 512:(kc + 1) * 512],
                        start=(kc == 0), stop=(kc == 15)))
                V_YMM = te.v
                ve.wait(te, V_YMM)
                ve.wait(ve, ve.v)
                ve.emit(lambda e: e.reciprocal(rro[0:1, :], ps_d[:, :]))
                te.wait(ve, ve.v)
                V_RB = te.emit(lambda e: e.matmul(
                    ps_rb[:, :], ones_r[0:1, :], rro[0:1, :],
                    start=True, stop=True))
                ve.wait(te, V_RB)
                ve.emit(lambda e: e.tensor_copy(rbc[:, :], ps_rb[:, :]))
                ve.wait(ve, ve.v)
                ve.emit(lambda e, h=h, qh=qh: e.tensor_mul(
                    ySB[:, h * TPC + qh * 512: h * TPC + qh * 512 + 512],
                    ps_y[:, :], rbc[:, :]))
        V_ATT = ve.v
        pha2.close()

        for eng in (te, sc, gp):
            eng.wait(ve, V_ATT)
        emit_ffcols(p, nc, pha, ySB, TPC, woD2, None, TPC, None, NCH,
                    tag="zw", addto=xsb, addW=TPC)
        V_X7 = ve.v
        pha.close()

        V_X8 = emit_ffn1(p, nc, ctx, xsb, consts, wr16T, sgD, suD, sdD,
                         gwD, uwD, dwD, [(ve, V_X7)], tag="Z")
        sy.wait(ve, V_X8)
        sy.dma(lambda e: e.dma_start(
            out=x8T.rearrange("(cc p) t -> p cc t", p=128),
            in_=xsb.rearrange("p (cc t) -> p cc t", cc=NCH)))
        sy.wait(sy, sy.v)
        p.finish()
    return nc



def _scramble(att_cores):
    outs = []
    m = np.arange(T) % 128
    hh = np.arange(T) // 128
    j = np.arange(16)
    for b in range(B):
        A = np.concatenate([att_cores[2 * b], att_cores[2 * b + 1]],
                           axis=0).reshape(T, 16, 128)
        R = A[16 * m[:, None] + j[None, :], hh[:, None], :].reshape(T, D)
        for h in range(2):
            outs.append(np.ascontiguousarray(R[h * TPC:(h + 1) * TPC].T))
    return outs


def _rope_tab(positions):
    inv = 1.0 / 10000.0 ** (np.arange(0, 128, 2, dtype=np.float32) / 128.0)
    fr = np.outer(positions.astype(np.float32), inv)
    W = len(positions)
    tab = np.zeros((128, 2 * W), np.float32)
    tab[0:64, 0:W] = np.cos(fr).T
    tab[64:128, 0:W] = np.cos(fr).T
    tab[0:64, W:] = np.sin(fr).T
    tab[64:128, W:] = np.sin(fr).T
    return tab


def _perm_headcols(Wm):
    WT = np.ascontiguousarray(np.asarray(Wm, np.float32).T)
    idx = np.zeros(2048, np.int64)
    for h in range(16):
        idx[h * 128:h * 128 + 64] = h * 128 + np.arange(0, 128, 2)
        idx[h * 128 + 64:(h + 1) * 128] = h * 128 + np.arange(1, 128, 2)
    return np.ascontiguousarray(WT[:, idx])


def _build_maskB(h):
    m = np.zeros((16, 128, 256), np.float32)
    for ci in range(4):
        gc = h * 4 + ci
        for kc in range(4):
            for jj in range(128):
                ww = kc * 128 + jj
                if ww < 256:
                    m[ci * 4 + kc, jj, :] = 0.0 if gc > 0 else -30000.0
                else:
                    qq = np.arange(256)
                    m[ci * 4 + kc, jj, :] = np.where(ww - 256 <= qq, 0.0,
                                                     -30000.0)
    return np.ascontiguousarray(m.transpose(1, 0, 2).reshape(128, 16 * 256))


def _prep_attnA1(inputs):
    x = np.asarray(inputs["L0_attn_a_x"], np.float32)
    wqkvT = np.ascontiguousarray(np.asarray(inputs["L0_attn_a_w_qkv"],
                                            np.float32).T)
    mask = np.full((128, 128), -30000.0, np.float32)
    for t in range(8):
        mask[t * 16:(t + 1) * 16, t * 16:(t + 1) * 16] = 0.0
    ident = np.eye(128, dtype=np.float32)
    onescol = np.ones((128, 2), np.float32)
    onescol[:, 1] = 1e-5
    onesrow = np.ones((1, 128), np.float32)
    maps = []
    for c in range(NCORES):
        b, h = c // 2, c % 2
        xTc = np.ascontiguousarray(x[b, h * TPC:(h + 1) * TPC, :].T)
        maps.append({"xT": xTc, "wqkvT": wqkvT, "mask": mask, "identb": ident,
                     "onescol": onescol, "onesrow": onesrow})
    return maps


def _scat_input(parts):
    P = np.stack([np.asarray(q, np.float32) for q in parts], axis=0)
    return np.ascontiguousarray(P.transpose(2, 1, 0).reshape(2048, 64))


def kernel(**inputs):
    inputs = {k: np.asarray(v) for k, v in inputs.items()}
    T4 = lambda a: np.ascontiguousarray(np.asarray(a, np.float32).T)
    DC = lambda a: np.ascontiguousarray(np.asarray(a, np.float32))
    onescol = np.ones((128, 4), np.float32)
    onescol[:, 1] = 1e-5
    onescol[:, 2] = -12.0
    onesrow = np.ones((1, 128), np.float32)
    ident = np.eye(128, dtype=np.float32)
    zscat = np.zeros((2048, 64), np.float32)
    cores = list(range(NCORES))

    maps = _prep_attnA1(inputs)
    res = run_bass_kernel_spmd(build_attnA1(), maps, core_ids=cores)
    att = [np.asarray(res.results[c]["att"]) for c in cores]
    rows = _scramble(att)

    x = np.asarray(inputs["L0_attn_a_x"], np.float32)
    ffnw = {"w1T": T4(inputs["L0_ffn_w1"]), "w3T": T4(inputs["L0_ffn_w3"]),
            "w2T": T4(inputs["L0_ffn_w2"]), "wrT": T4(inputs["L0_ffn_wr"])}
    maps = []
    for c in cores:
        b, h = c // 2, c % 2
        maps.append({"xT": np.ascontiguousarray(x[b, h * TPC:(h + 1) * TPC, :].T),
                     "rowsT": rows[c], "wprojT": T4(inputs["L0_attn_a_w_proj"]),
                     "onescol": onescol, "onesrow": onesrow, **ffnw})
    res = run_bass_kernel_spmd(build_F1(), maps, core_ids=cores)
    x1T = [np.asarray(res.results[c]["x1T"]) for c in cores]
    parts = [np.asarray(res.results[c]["part"]) for c in cores]

    scat1 = _scat_input(parts)
    maps = []
    for c in cores:
        b, h = c // 2, c % 2
        if h == 1:
            halo = np.ascontiguousarray(x1T[2 * b][:, 768:1024])
            kpos = np.arange(768, 2048)
        else:
            halo = np.zeros((2048, 256), np.float32)
            kpos = np.concatenate([np.zeros(256, np.int64),
                                   np.arange(0, 1024)])
        maps.append({
            "xT": x1T[c], "scat": scat1 if c == 0 else zscat, "haloT": halo,
            "wqTp": _perm_headcols(inputs["L0_attn_b_Wq"]),
            "wkTp": _perm_headcols(inputs["L0_attn_b_Wk"]),
            "wvT": T4(inputs["L0_attn_b_Wv"]),
            "woD": T4(inputs["L0_attn_b_Wo"]),
            "tabQ": _rope_tab(np.arange(h * 1024, (h + 1) * 1024)),
            "tabK": _rope_tab(kpos), "maskB": _build_maskB(h),
            "identb": ident, "onescol": onescol, "onesrow": onesrow, **ffnw})
    res = run_bass_kernel_spmd(build_B1(), maps, core_ids=cores)
    x3T = [np.asarray(res.results[c]["x3T"]) for c in cores]
    parts = [np.asarray(res.results[c]["part"]) for c in cores]

    scat2 = _scat_input(parts)
    maps = [{"xT": x3T[c], "scat": scat2 if c == 0 else zscat,
             "onescol": onescol, "onesrow": onesrow} for c in cores]
    res = run_bass_kernel_spmd(build_M1(), maps, core_ids=cores)
    x4T = [np.asarray(res.results[c]["x4T"]) for c in cores]
    xhT = [np.asarray(res.results[c]["xhT"]) for c in cores]

    maps = []
    for c in cores:
        b, eh = c // 2, c % 2
        permv = np.concatenate([np.arange(eh * 1024, (eh + 1) * 1024),
                                np.arange((1 - eh) * 1024, (2 - eh) * 1024)])
        own = permv[:1024]
        xf = np.concatenate([xhT[2 * b], xhT[2 * b + 1]], axis=1)
        cwp = np.asarray(inputs["L1_attn_a_conv_w"], np.float32)[permv]
        cbp = np.asarray(inputs["L1_attn_a_conv_b"], np.float32)[permv]
        xp = np.asarray(inputs["L1_attn_a_x_proj_w"], np.float32)
        dtw = np.asarray(inputs["L1_attn_a_dt_proj_w"], np.float32)
        dtb = np.asarray(inputs["L1_attn_a_dt_proj_b"], np.float32)[own]
        Al = np.asarray(inputs["L1_attn_a_A_log"], np.float32)[own]
        Dv = np.asarray(inputs["L1_attn_a_D"], np.float32)[own]
        maps.append({
            "xh": np.ascontiguousarray(xf[permv]),
            "cw": np.ascontiguousarray(cwp),
            "cb": np.ascontiguousarray(cbp.reshape(16, 128).T),
            "xpT": np.ascontiguousarray(xp[:, permv].T),
            "dtwT": np.ascontiguousarray(dtw[own].T),
            "dtb": np.ascontiguousarray(dtb.reshape(8, 128).T),
            "Ain": np.ascontiguousarray(
                Al.reshape(8, 128, 16).transpose(1, 0, 2).reshape(128, 128)),
            "Dvin": np.ascontiguousarray(Dv.reshape(8, 128).T),
            "onescol": onescol, "onesrow": onesrow})
    res = run_bass_kernel_spmd(build_M2(), maps, core_ids=cores)
    ym = [np.asarray(res.results[c]["yout"]) for c in cores]

    moew = {"wr16T": T4(inputs["L1_ffn_router_w"]),
            "sgD": DC(inputs["L1_ffn_shared_gate_w"]),
            "suD": DC(inputs["L1_ffn_shared_up_w"]),
            "sdD": DC(inputs["L1_ffn_shared_down_w"]),
            "gwD": np.ascontiguousarray(np.asarray(
                inputs["L1_ffn_gate_w"], np.float32).reshape(16 * D, TPC)),
            "uwD": np.ascontiguousarray(np.asarray(
                inputs["L1_ffn_up_w"], np.float32).reshape(16 * D, TPC)),
            "dwD": np.ascontiguousarray(np.asarray(
                inputs["L1_ffn_down_w"], np.float32).reshape(16 * TPC, D)),
            "identb": ident, "onescol": onescol, "onesrow": onesrow}
    maps = []
    for c in cores:
        b, h = c // 2, c % 2
        yTc = np.ascontiguousarray(np.concatenate(
            [ym[2 * b][:, h * TPC:(h + 1) * TPC],
             ym[2 * b + 1][:, h * TPC:(h + 1) * TPC]], axis=0))
        maps.append({"xT": x4T[c], "yT": yTc,
                     "wqD": DC(inputs["L1_attn_b_Wq"]),
                     "wkD": DC(inputs["L1_attn_b_Wk"]),
                     "wvD": DC(inputs["L1_attn_b_Wv"]), **moew})
    res = run_bass_kernel_spmd(build_F2(), maps, core_ids=cores)
    x6T = [np.asarray(res.results[c]["x6T"]) for c in cores]
    qts = [np.asarray(res.results[c]["qTo"]) for c in cores]
    kts = [np.asarray(res.results[c]["kTo"]) for c in cores]
    vs = [np.asarray(res.results[c]["vO"]) for c in cores]

    maps = []
    for c in cores:
        b, h = c // 2, c % 2
        maps.append({"xT": x6T[c], "qTd": qts[c],
                     "kTf": np.ascontiguousarray(np.concatenate(
                         [kts[2 * b], kts[2 * b + 1]], axis=1)),
                     "vf": np.ascontiguousarray(np.concatenate(
                         [vs[2 * b], vs[2 * b + 1]], axis=0)),
                     "woD2": DC(inputs["L1_attn_b_Wo"]), **moew})
    res = run_bass_kernel_spmd(build_Z(), maps, core_ids=cores)

    out = np.zeros((B, T, D), np.float32)
    for c in cores:
        b, h = c // 2, c % 2
        out[b, h * TPC:(h + 1) * TPC, :] = np.asarray(
            res.results[c]["x8T"]).T
    return out
